# version 03
# baseline (speedup 1.0000x reference)
import numpy as np
import ml_dtypes
import concourse.bass as bass
import concourse.mybir as mybir
from concourse.bass_utils import run_bass_kernel_spmd

F32 = mybir.dt.float32
BF16 = mybir.dt.bfloat16
AF = mybir.ActivationFunctionType
ALU = mybir.AluOpType
AX = mybir.AxisListType
BF = ml_dtypes.bfloat16

T = 1024
EMB = 4096
NEC = 32
HID = 11008
NHC = 86
D1 = 512
NDC = 4
DFF = 2048
NFC = 16
DEPTH1 = 4
SCALE_QK = float(1.0 / np.float32(np.sqrt(np.float32(128.0))))
EPS = 1e-5

FUSED = True
DEBUG_G5 = False
GROUP_SETS = [["g1"], ["g2"], ["g3"], ["g4"], ["g5"], ["g6"]]


class Ctx:
    def __init__(self, nc):
        self.nc = nc
        self.block = None
        self.sem = {}
        self.cnt = {}
        self.t = {}
        self.d = {}
        self.guards = {}

    def S(self, name):
        if name not in self.sem:
            self.sem[name] = self.nc.alloc_semaphore(name)
            self.cnt[name] = 0
        return self.sem[name]

    def bump(self, name, amt):
        self.S(name)
        self.cnt[name] += amt
        return self.cnt[name]

    def step(self, *parts):
        merged, order = {}, []
        for en, f in parts:
            if f is None:
                continue
            if en in merged:
                old = merged[en]
                merged[en] = (lambda e, c, a=old, b=f: (a(e, c), b(e, c))[1])
            else:
                merged[en] = f
                order.append(en)
        if not order:
            return
        base = self.cnt.get("step", 0)
        self.S("step")
        for en in order:
            def body(e, fn=merged[en], base=base):
                if base:
                    e.wait_ge(self.S("step"), base)
                last = fn(e, self)
                if last is None:
                    last = _mark(e)
                last.then_inc(self.S("step"), 1)

            getattr(self.block, en)(body)
        self.cnt["step"] = base + len(order)

    def sb(self, name, shape, dtype, side=None):
        g = self.nc.sbuf_tensor(name, shape, dtype, side=side)
        h = g.__enter__()
        self.guards[name] = (len(self.guards), g)
        return h

    def ps(self, name, shape, dtype=F32):
        g = self.nc.psum_tensor(name, shape, dtype)
        h = g.__enter__()
        self.guards[name] = (len(self.guards), g)
        return h

    def free(self, *names):
        for nm in sorted(names, key=lambda n: -self.guards[n][0]):
            self.guards.pop(nm)[1].__exit__(None, None, None)



def _mark(e):
    try:
        return e.engine_nop()
    except AttributeError:
        return e.nop()


def VG(e, c, instr, extra=False):
    c.S("vs")
    c.cnt["vs"] += 1
    instr.then_inc(c.S("vs"), 1)
    e.wait_ge(c.S("vs"), c.cnt["vs"])
    if extra:
        n = _mark(e)
        c.cnt["vs"] += 1
        n.then_inc(c.S("vs"), 1)
        e.wait_ge(c.S("vs"), c.cnt["vs"])
    return instr


def wdma(e, c, name):
    e.wait_ge(c.S(name), c.cnt.get(name, 0))


def host_prep(inp):
    H = {}
    c = np.ascontiguousarray
    x = np.asarray(inp["L0_attn_x"], dtype=np.float32)
    H["x"] = [c(x[b, h * T:(h + 1) * T, :]) for b in range(4) for h in range(2)]

    wq = np.asarray(inp["L0_attn_Wq"], np.float32)
    H["WqJ"] = c(wq.reshape(EMB, 128, 8, 4).transpose(3, 0, 2, 1).reshape(4, EMB, 1024))
    H["bq"] = c(np.asarray(inp["L0_attn_bq"], np.float32))
    H["WkG"] = c(np.asarray(inp["L0_attn_Wk"], np.float32).transpose(2, 0, 1))
    H["bk"] = c(np.asarray(inp["L0_attn_bk"], np.float32))
    H["WvG"] = c(np.asarray(inp["L0_attn_Wv"], np.float32).transpose(2, 0, 1))
    H["bv"] = c(np.asarray(inp["L0_attn_bv"], np.float32))

    H["w1c"] = c(np.asarray(inp["L0_ffn_w1"], np.float32).T
                 .reshape(EMB, NHC, 128).transpose(1, 0, 2))
    H["w3c"] = c(np.asarray(inp["L0_ffn_w3"], np.float32).T
                 .reshape(EMB, NHC, 128).transpose(1, 0, 2))
    H["w2s"] = c(np.asarray(inp["L0_ffn_w2"], np.float32).T
                 .reshape(HID, 8, 512).transpose(1, 0, 2))
    H["A1T"] = c(np.asarray(inp["L0_ffn_lora_A1"], np.float32).T)
    H["B1T"] = c(np.asarray(inp["L0_ffn_lora_B1"], np.float32).T)
    H["A3T"] = c(np.asarray(inp["L0_ffn_lora_A3"], np.float32).T)
    H["B3T"] = c(np.asarray(inp["L0_ffn_lora_B3"], np.float32).T)
    H["A2T"] = c(np.asarray(inp["L0_ffn_lora_A2"], np.float32).T)
    H["B2T"] = c(np.asarray(inp["L0_ffn_lora_B2"], np.float32).T)

    H["WinT"] = c(np.asarray(inp["L1_attn_Win"], np.float32).T)
    for nm in ("ln1_w", "ln1_b", "phz_re", "phz_im", "lc_re", "lc_im",
               "sc_lin", "ln2_w", "ln2_b", "b2"):
        H["p_" + nm] = c(np.asarray(inp["L1_attn_" + nm], np.float32).reshape(16, 128))
    H["p_b1"] = c(np.asarray(inp["L1_attn_b1"], np.float32).reshape(64, 128))
    H["p_bin"] = c(np.asarray(inp["L1_attn_bin"], np.float32).reshape(4, 128))
    H["p_lnl_w"] = c(np.asarray(inp["L1_attn_lnl_w"], np.float32).reshape(4, 128))
    H["p_lnl_b"] = c(np.asarray(inp["L1_attn_lnl_b"], np.float32).reshape(4, 128))
    H["W1T4"] = c(np.asarray(inp["L1_attn_W1"], np.float32).transpose(0, 2, 1))
    H["W2T4"] = c(np.asarray(inp["L1_attn_W2"], np.float32).transpose(0, 2, 1))
    H["WoutT"] = c(np.asarray(inp["L1_attn_Wout"], np.float32).T)
    H["bout_row"] = c(np.asarray(inp["L1_attn_bout"], np.float32).reshape(1, EMB))

    H["WbT"] = c(np.asarray(inp["L1_ffn_W_base"], np.float32).T
                 .reshape(EMB, 8, 512).transpose(1, 0, 2))
    H["bb_row"] = c(np.asarray(inp["L1_ffn_b_base"], np.float32).reshape(1, EMB))
    H["WgT"] = c(np.asarray(inp["L1_ffn_W_gate"], np.float32).T)
    H["AallT"] = c(np.asarray(inp["L1_ffn_A"], np.float32).reshape(128, EMB).T)
    H["BallP"] = c(np.asarray(inp["L1_ffn_B"], np.float32).transpose(0, 2, 1).reshape(128, EMB))

    H["is2"] = [np.full((128, 1), float(core % 2), np.float32) for core in range(8)]
    return H


G_INPUTS = {
    "g1": ["WqJ", "bq", "WkG", "bk", "WvG", "bv"],
    "g2": [],
    "g3": ["w1c", "w3c", "A1T", "B1T", "A3T", "B3T"],
    "g4": ["w2s", "B2T", "A2T"],
    "g5": ["WinT", "p_ln1_w", "p_ln1_b", "p_phz_re", "p_phz_im", "p_lc_re",
           "p_lc_im", "p_sc_lin", "p_ln2_w", "p_ln2_b", "p_b1", "p_b2",
           "p_bin", "p_lnl_w", "p_lnl_b", "W1T4", "W2T4", "WoutT", "bout_row"],
    "g6": ["WbT", "bb_row", "WgT", "AallT", "BallP"],
}
HANDOFFS = {
    "xn_bf": ([T, EMB], BF16),
    "qg": ([8, 128, T], BF16),
    "kg": ([8, 128, T], BF16),
    "vg": ([8, 128, T], BF16),
    "negc8": ([1024], BF16),
    "s0": ([T, 1024], BF16),
    "gdram": ([HID, T], BF16),
    "h2lo_d": ([16, T], F32),
    "xnew_d": ([T, EMB], F32),
    "xn1_bf": ([T, EMB], BF16),
    "s1_d": ([T, EMB], BF16),
}
G_READS = {
    "g1": ["x"], "g2": ["qg", "kg", "vg", "negc8"],
    "g3": ["xn_bf"], "g4": ["gdram", "s0", "x"],
    "g5": ["xnew_d"], "g6": ["xn1_bf", "s1_d", "xnew_d"],
}
G_WRITES = {
    "g1": ["xn_bf", "qg", "kg", "vg", "negc8"],
    "g2": ["s0"], "g3": ["gdram"], "g4": ["xnew_d", "h2lo_d"],
    "g5": ["xn1_bf", "s1_d"], "g6": [],
}


def decl_io(c, groups, host, fused):
    nc = c.nc
    reads, writes, weights = set(), set(), []
    for g in groups:
        weights += [k for k in G_INPUTS[g] if k not in weights]
        reads |= set(G_READS[g])
        writes |= set(G_WRITES[g])
    for k in weights:
        c.d[k] = nc.dram_tensor(k, list(host[k].shape), F32, kind="ExternalInput").ap()
    if "x" in reads:
        c.d["x"] = nc.dram_tensor("x", [T, EMB], F32, kind="ExternalInput").ap()
    if "g5" in groups:
        c.d["is2"] = nc.dram_tensor("is2", [128, 1], F32, kind="ExternalInput").ap()
    if "g6" in groups:
        c.d["out"] = nc.dram_tensor("out", [T, EMB], F32, kind="ExternalOutput").ap()
    if DEBUG_G5 and "g5" in groups:
        c.d["h_dbg"] = nc.dram_tensor("h_dbg", [128, 10 * NDC * T], F32,
                                      kind="ExternalOutput").ap()
        c.d["s_dbg"] = nc.dram_tensor("s_dbg", [128, 8 * T], F32,
                                      kind="ExternalOutput").ap()
    for nm in (reads | writes) - {"x"}:
        shape, dt = HANDOFFS[nm]
        if fused:
            c.d[nm] = nc.dram_tensor(nm, shape, dt).ap()
        else:
            kind = "ExternalOutput" if nm in writes else "ExternalInput"
            c.d[nm] = nc.dram_tensor(nm, shape, dt, kind=kind).ap()


def ln_global(c, src_name, dst_name, tag):
    xt = [c.sb(tag + "_xt" + str(i), [128, EMB], F32) for i in range(2)]
    sq = c.sb(tag + "_sq", [128, EMB], F32)
    st = c.sb(tag + "_st", [128, 8], F32)
    xnt = [c.sb(tag + "_xnt" + str(i), [128, EMB], BF16) for i in range(2)]

    def load(e, c, tc):
        par = "x" + str(tc % 2)
        e.dma_start(out=xt[tc % 2][:],
                    in_=c.d[src_name][tc * 128:(tc + 1) * 128, :]
                    ).then_inc(c.S(par), 16)
        c.bump(par, 16)
        return _mark(e)

    c.step(("sync", lambda e, c: load(e, c, 0)))
    for tc in range(8):
        par = "x" + str(tc % 2)

        def sums(e, c, tc=tc, par=par):
            wdma(e, c, par)
            return e.activation(sq[:], xt[tc % 2][:], AF.Square)

        def rsum(e, c, tc=tc, par=par):
            wdma(e, c, par)
            VG(e, c, e.tensor_reduce(st[:, 0:1], xt[tc % 2][:], AX.X, ALU.add),
               extra=True)
            return None

        nxt = (lambda e, c, tc=tc: load(e, c, tc + 1)) if tc < 7 else None
        c.step(("scalar", sums), ("vector", rsum), ("sync", nxt))

        def stats(e, c, tc=tc):
            VG(e, c, e.tensor_reduce(st[:, 4:5], sq[:], AX.X, ALU.add), extra=True)
            VG(e, c, e.tensor_scalar(st[:, 2:3], st[:, 0:1], -1.0 / EMB, None, ALU.mult))
            VG(e, c, e.tensor_scalar(st[:, 3:4], st[:, 4:5], 1.0 / EMB, None, ALU.mult))
            VG(e, c, e.tensor_tensor(st[:, 1:2], st[:, 2:3], st[:, 2:3], ALU.mult))
            VG(e, c, e.tensor_tensor(st[:, 5:6], st[:, 3:4], st[:, 1:2], ALU.subtract))
            VG(e, c, e.tensor_scalar(st[:, 5:6], st[:, 5:6], EPS, None, ALU.add))
            return None

        c.step(("vector", stats))
        c.step(("scalar", lambda e, c: VG(e, c, e.activation(st[:, 6:7],
                                                              st[:, 5:6], AF.Sqrt)) and None))

        def norm(e, c, tc=tc):
            wdma(e, c, "s" + str(tc % 2))
            VG(e, c, e.reciprocal(st[:, 7:8], st[:, 6:7]), extra=True)
            return e.tensor_scalar(xnt[tc % 2][:], xt[tc % 2][:], st[:, 2:3],
                                   st[:, 7:8], ALU.add, ALU.mult)

        c.step(("vector", norm))

        def store(e, c, tc=tc):
            par = "s" + str(tc % 2)
            e.dma_start(out=c.d[dst_name][tc * 128:(tc + 1) * 128, :],
                        in_=xnt[tc % 2][:]).then_inc(c.S(par), 16)
            c.bump(par, 16)
            return _mark(e)

        c.step(("gpsimd", store))

    def fin(e, c):
        wdma(e, c, "s0")
        wdma(e, c, "s1")
        return _mark(e)

    c.step(("gpsimd", fin))
    c.free(*(tag + "_xt" + str(i) for i in range(2)), tag + "_sq", tag + "_st",
           *(tag + "_xnt" + str(i) for i in range(2)))


def load_xnT(c, src_name):
    xnT = c.t["xnT"]

    def tp(e, c):
        for ec in range(NEC):
            e.dma_start(out=xnT[:, ec * T:(ec + 1) * T],
                        in_=c.d[src_name][:, ec * 128:(ec + 1) * 128],
                        transpose=True).then_inc(c.S("tp"), 16)
            c.bump("tp", 16)
        wdma(e, c, "tp")
        return _mark(e)

    c.step(("sync", tp))


def build_g1(c):
    nc = c.nc
    ln_global(c, "x", "xn_bf", "l0")
    load_xnT(c, "xn_bf")

    xnT = c.t["xnT"]
    wq4 = [c.sb(f"g1_wq4{i}", [128, 4096], BF16) for i in range(2)]
    wqt = c.sb("g1_wqt", [128, NEC * 1024], BF16)
    wkb = [c.sb(f"g1_wkb{i}", [128, 4096], BF16) for i in range(2)]
    bqs = c.sb("g1_bqs", [128, 32], F32)
    bq8 = c.sb("g1_bq8", [128, 8], F32)
    bkv = c.sb("g1_bkv", [128, 16], F32)
    evac = [c.sb(f"g1_evac{i}", [128, 1024], BF16) for i in range(3)]
    sqs = [c.sb(f"g1_sqs{i}", [128, 1024], F32) for i in range(2)]
    qacc = c.sb("g1_qacc", [128, T], F32)
    kacc = c.sb("g1_kacc", [128, T], F32)
    accb = c.sb("g1_accb", [128, T], BF16)
    mxq = c.sb("g1_mxq", [128, 8], F32)
    mxk = c.sb("g1_mxk", [128, 8], F32)
    negc_c = c.sb("g1_negc", [128, 8], BF16)
    negcT = c.sb("g1_negcT", [8, 128], BF16)
    ident = c.sb("g1_id", [128, 128], BF16)
    ps_a = c.ps("g1_psa", [128, 1024])
    ps_b = c.ps("g1_psb", [128, 1024])
    ps_t = c.ps("g1_pst", [128, 512], BF16)
    id_d = nc.inline_tensor(np.eye(128, dtype=BF), name="ident128").ap()

    def prolog(e, c):
        e.dma_start(out=bqs[:], in_=c.d["bq"]).then_inc(c.S("pl"), 16)
        e.dma_start(out=bkv[:, 0:8], in_=c.d["bk"]).then_inc(c.S("pl"), 16)
        e.dma_start(out=bkv[:, 8:16], in_=c.d["bv"]).then_inc(c.S("pl"), 16)
        e.dma_start(out=ident[:], in_=id_d).then_inc(c.S("pl"), 16)
        c.bump("pl", 64)
        return _mark(e)

    def wqload(e, c, ec):
        par = "w" + str(ec % 2)
        for j in range(4):
            e.dma_start(out=wq4[ec % 2][:, j * 1024:(j + 1) * 1024],
                        in_=c.d["WqJ"][j, ec * 128:(ec + 1) * 128, :]
                        ).then_inc(c.S(par), 16)
        c.bump(par, 64)
        return _mark(e)

    c.step(("gpsimd", prolog))
    c.step(("gpsimd", lambda e, c: wqload(e, c, 0)))

    def bqr(e, c):
        wdma(e, c, "pl")
        VG(e, c, e.tensor_reduce(bq8[:].rearrange("p (g o) -> p g o", g=8),
                                 bqs[:].rearrange("p (g j) -> p g j", g=8),
                                 AX.X, ALU.add), extra=True)
        VG(e, c, e.tensor_scalar(bq8[:], bq8[:], SCALE_QK, None, ALU.mult))
        e.memset(qacc[:], 0.0)
        return e.memset(kacc[:], 0.0)

    c.step(("vector", bqr))

    for ec in range(NEC):
        def red(e, c, ec=ec):
            wdma(e, c, "w" + str(ec % 2))
            VG(e, c, e.tensor_reduce(
                wqt[:, ec * 1024:(ec + 1) * 1024],
                wq4[ec % 2][:].rearrange("p (j gd) -> p gd j", j=4),
                AX.X, ALU.add), extra=True)
            return None

        c.step(("vector", red),
               ("gpsimd", (lambda e, c, ec=ec: wqload(e, c, ec + 1))
                if ec + 1 < NEC else None))

    def kvload(e, c, idx):
        wname = "WkG" if idx < 8 else "WvG"
        par = "w" + str(idx % 2)
        e.dma_start(out=wkb[idx % 2][:].rearrange("p (ec col) -> p ec col", col=128),
                    in_=c.d[wname][idx % 8].rearrange("(ec p) col -> p ec col", p=128)
                    ).then_inc(c.S(par), 16)
        c.bump(par, 16)
        return _mark(e)

    c.step(("gpsimd", lambda e, c: kvload(e, c, 0)))

    NP = 24

    def mms(e, c, i):
        if i < 16:
            wdma(e, c, "w" + str(i % 2))
            w = wkb[i % 2]
            cols = [w[:, ec * 128:(ec + 1) * 128] for ec in range(NEC)]
        else:
            g = i - 16
            cols = [wqt[:, ec * 1024 + g * 128: ec * 1024 + (g + 1) * 128]
                    for ec in range(NEC)]
        ps = ps_a if i % 2 == 0 else ps_b
        r = None
        for th in range(2):
            for ec in range(NEC):
                r = e.matmul(ps[:, th * 512:(th + 1) * 512], cols[ec],
                             xnT[:, ec * T + th * 512: ec * T + (th + 1) * 512],
                             start=(ec == 0), stop=(ec == NEC - 1))
        return r

    def ev(e, c, i):
        ps = ps_a if i % 2 == 0 else ps_b
        wdma(e, c, "so")
        if i < 16:
            return e.tensor_scalar(evac[i % 3][:], ps[:], bkv[:, i:i + 1], None,
                                   ALU.add)
        return e.tensor_scalar(evac[i % 3][:], ps[:], SCALE_QK,
                               bq8[:, i - 16:i - 15], ALU.mult, ALU.add)

    def sq(e, c, i):
        if i < 8 or i >= 16:
            return e.activation(sqs[i % 2][:], evac[i % 3][:], AF.Square)
        return None

    def acc(e, c, i):
        dst = kacc if i < 8 else qacc
        return e.tensor_tensor(dst[:], dst[:], sqs[i % 2][:], ALU.add)

    def store(e, c, i):
        dsts = {0: "kg", 1: "vg", 2: "qg"}
        e.dma_start(out=c.d[dsts[i // 8]][i % 8], in_=evac[i % 3][:]
                    ).then_inc(c.S("so"), 16)
        c.bump("so", 16)
        return _mark(e)

    for i in range(NP + 3):
        parts = []
        if i < NP:
            parts.append(("tensor", lambda e, c, i=i: mms(e, c, i)))
            if i + 1 < 16:
                parts.append(("gpsimd", lambda e, c, i=i: kvload(e, c, i + 1)))
        if 0 <= i - 1 < NP:
            parts.append(("vector", lambda e, c, i=i: ev(e, c, i - 1)))
        if 0 <= i - 2 < NP and (i - 2 < 8 or i - 2 >= 16):
            parts.append(("scalar", lambda e, c, i=i: sq(e, c, i - 2)))
        if 0 <= i - 2 < NP:
            parts.append(("gpsimd", lambda e, c, i=i: store(e, c, i - 2)))
        if 0 <= i - 3 < NP and (i - 3 < 8 or i - 3 >= 16):
            parts.append(("vector", lambda e, c, i=i: acc(e, c, i - 3)))
        c.step(*parts)

    c.step(("vector", lambda e, c: e.tensor_copy(accb[:], qacc[:])))
    for src_i in range(2):
        for half in range(2):
            def tps(e, c, half=half):
                r = None
                for blk in range(4):
                    r = e.transpose(ps_t[:, blk * 128:(blk + 1) * 128],
                                    accb[:, (half * 4 + blk) * 128:
                                         (half * 4 + blk + 1) * 128], ident[:])
                return r

            c.step(("tensor", tps))

            def mxs(e, c, src_i=src_i, half=half):
                dst = mxq if src_i == 0 else mxk
                for blk in range(4):
                    VG(e, c, e.tensor_reduce(dst[:, half * 4 + blk:half * 4 + blk + 1],
                                             ps_t[:, blk * 128:(blk + 1) * 128],
                                             AX.X, ALU.max), extra=True)
                if src_i == 0 and half == 1:
                    return e.tensor_copy(accb[:], kacc[:])
                return None

            c.step(("vector", mxs))

    c.step(("vector", lambda e, c: VG(e, c, e.tensor_tensor(mxq[:], mxq[:],
                                                             mxk[:], ALU.mult)) and None))
    c.step(("scalar", lambda e, c: VG(e, c, e.activation(mxk[:], mxq[:], AF.Sqrt)) and None))
    c.step(("vector", lambda e, c: VG(e, c, e.tensor_scalar(negc_c[:], mxk[:], -1.0,
                                                            None, ALU.mult)) and None))
    c.step(("tensor", lambda e, c: e.transpose(ps_t[0:8, 0:128], negc_c[:], ident[:])))
    c.step(("vector", lambda e, c: VG(e, c, e.tensor_copy(negcT[:], ps_t[0:8, 0:128])) and None))

    def nstore(e, c):
        e.dma_start(out=c.d["negc8"].rearrange("(a b) -> a b", a=8),
                    in_=negcT[:]).then_inc(c.S("so"), 16)
        c.bump("so", 16)
        wdma(e, c, "so")
        return _mark(e)

    c.step(("gpsimd", nstore))
    c.free("g1_wq40", "g1_wq41", "g1_wqt", "g1_wkb0", "g1_wkb1", "g1_bqs",
           "g1_bq8", "g1_bkv", "g1_evac0", "g1_evac1", "g1_evac2", "g1_sqs0", "g1_sqs1",
           "g1_qacc", "g1_kacc", "g1_accb", "g1_mxq", "g1_mxk", "g1_negc",
           "g1_negcT", "g1_id", "g1_psa", "g1_psb", "g1_pst")


def build_g2(c):
    nc = c.nc
    GRP = 16
    NG = T // GRP
    NSB = NG * 4

    crow = c.sb("g2_crow", [1, T], BF16)
    cbc = c.sb("g2_cbc", [128, T], F32)
    onesr = c.sb("g2_ones", [1, 128], BF16)
    ident = c.sb("g2_id", [128, 128], BF16)
    id_d = nc.inline_tensor(np.eye(128, dtype=BF), name="g2ident").ap()
    ones_d = nc.inline_tensor(np.ones((1, 128), dtype=BF), name="g2ones").ap()
    ps_c = c.ps("g2_psc", [128, 1024])

    def prolog(e, c):
        e.dma_start(out=ident[:], in_=id_d).then_inc(c.S("pl"), 16)
        e.dma_start(out=onesr[:], in_=ones_d).then_inc(c.S("pl"), 16)
        e.dma_start(out=crow[:], in_=c.d["negc8"].rearrange("(a b) -> a b", a=1)
                    ).then_inc(c.S("pl"), 16)
        c.bump("pl", 48)
        return _mark(e)

    c.step(("gpsimd", prolog))

    def cb_mm(e, c):
        wdma(e, c, "pl")
        e.matmul(ps_c[:, 0:512], onesr[:], crow[:, 0:512], start=True, stop=True)
        return e.matmul(ps_c[:, 512:1024], onesr[:], crow[:, 512:1024],
                        start=True, stop=True)

    c.step(("tensor", cb_mm))
    c.step(("vector", lambda e, c: e.tensor_copy(cbc[:], ps_c[:])))
    c.free("g2_psc")

    qt = [c.sb(f"g2_qt{i}", [8, 128 * GRP], BF16) for i in range(3)]
    kt = [c.sb(f"g2_kt{i}", [8, 128 * GRP], BF16) for i in range(3)]
    vt = [c.sb(f"g2_vt{i}", [128, 9 * GRP], BF16) for i in range(3)]
    expt = [c.sb(f"g2_ex{i}", [128, 512], BF16) for i in range(2)]
    r32 = c.sb("g2_r32", [128, 32], F32)
    ot = c.sb("g2_ot", [128, T * 8], BF16)
    s0T = c.t["s0T"]
    ps_s = [c.ps(f"g2_pss{i}", [128, 512]) for i in range(2)]
    ps_o = [c.ps(f"g2_pso{i}", [128, 512]) for i in range(2)]
    ps_t = c.ps("g2_pst", [128, 512], BF16)

    def vinit(e, c):
        e.memset(vt[0][:], 1.0)
        e.memset(vt[1][:], 1.0)
        return e.memset(vt[2][:], 1.0)

    c.step(("vector", vinit))

    def qkvload(e, c, grp):
        b = grp % 3
        par = "w" + str(grp % 2)
        t0 = grp * GRP
        e.dma_start(out=qt[b][:].rearrange("g (d t) -> g d t", t=GRP),
                    in_=c.d["qg"][:, :, t0:t0 + GRP]).then_inc(c.S(par), 16)
        e.dma_start(out=kt[b][:].rearrange("g (d t) -> g d t", t=GRP),
                    in_=c.d["kg"][:, :, t0:t0 + GRP]).then_inc(c.S(par), 16)
        e.dma_start(out=vt[b][:].rearrange("e (ga t) -> e ga t", ga=9)[:, 0:8, :],
                    in_=c.d["vg"].transpose([1, 0, 2])[:, :, t0:t0 + GRP]
                    ).then_inc(c.S(par), 16)
        c.bump(par, 48)
        return _mark(e)

    c.step(("gpsimd", lambda e, c: qkvload(e, c, 0)))
    c.step(("gpsimd", lambda e, c: qkvload(e, c, 1)))

    def pe_fn(e, c, k):
        r = None
        if k < NSB:
            b = (k // 4) % 3
            if k % 4 == 0:
                wdma(e, c, "w" + str((k // 4) % 2))
            ps = ps_s[k % 2]
            for tok in range(4):
                tt = (k % 4) * 4 + tok
                kap = kt[b][:].rearrange("g (e t) -> g e t", t=GRP)[:, :, tt]
                qap = qt[b][:].rearrange("g (d t) -> g d t", t=GRP)[:, :, tt]
                r = e.matmul(ps[:, tok * 128:(tok + 1) * 128], kap, qap,
                             start=True, stop=True)
        kk = k - 2
        if 0 <= kk < NSB:
            b2 = (kk // 4) % 3
            po = ps_o[(kk // 8) % 2]
            for tok in range(4):
                tt = (kk % 4) * 4 + tok
                tw = (kk * 4 + tok) % 32
                vap = vt[b2][:].rearrange("e (ga t) -> e ga t", ga=9)[:, :, tt]
                r = e.matmul(po[:, tw * 9:tw * 9 + 9],
                             expt[kk % 2][:, tok * 128:(tok + 1) * 128], vap,
                             start=True, stop=True)
        return r

    def se_fn(e, c, k):
        kk = k - 1
        ps = ps_s[kk % 2]
        r = None
        for tok in range(4):
            tglob = kk * 4 + tok
            r = e.activation(expt[kk % 2][:, tok * 128:(tok + 1) * 128],
                             ps[:, tok * 128:(tok + 1) * 128], AF.Exp,
                             bias=cbc[:, tglob:tglob + 1])
        return r

    def norm_fn(e, c, k):
        w = (k - 3) // 8
        po = ps_o[w % 2]
        VG(e, c, e.reciprocal(r32[:, 0:32],
                              po[:, 0:288].rearrange("p (t n) -> p t n", t=32)[:, :, 8]),
           extra=True)
        return e.tensor_tensor(
            ot[:].rearrange("p (t g) -> p t g", t=T)[:, w * 32:(w + 1) * 32, :],
            po[:, 0:288].rearrange("p (t n) -> p t n", t=32)[:, :, 0:8],
            r32[:, 0:32].rearrange("p (t o) -> p t o", t=32).broadcast_to((128, 32, 8)),
            ALU.mult)

    for k in range(NSB + 4):
        parts = []
        if k < NSB or 0 <= k - 2 < NSB:
            parts.append(("tensor", lambda e, c, k=k: pe_fn(e, c, k)))
        if 0 <= k - 1 < NSB:
            parts.append(("scalar", lambda e, c, k=k: se_fn(e, c, k)))
        if 0 <= k - 3 < NSB and (k - 3) % 8 == 7:
            parts.append(("vector", lambda e, c, k=k: norm_fn(e, c, k)))
        if k < NSB and k % 4 == 0 and k // 4 + 2 < NG:
            parts.append(("gpsimd", lambda e, c, k=k: qkvload(e, c, k // 4 + 2)))
        c.step(*parts)

    for blk in range(4):
        for gpair in range(4):
            def tps(e, c, blk=blk, gpair=gpair):
                r = None
                for i in range(4):
                    g8 = gpair * 2 + i // 2
                    tb = i % 2
                    ap_in = (ot[:].rearrange("p (t g) -> p t g", t=T)
                             [:, blk * 256 + tb * 128: blk * 256 + (tb + 1) * 128, g8])
                    r = e.transpose(ps_t[:, i * 128:(i + 1) * 128], ap_in, ident[:])
                return r

            c.step(("tensor", tps))

            def evt(e, c, blk=blk, gpair=gpair):
                r = None
                for i in range(4):
                    g8 = gpair * 2 + i // 2
                    tb = i % 2
                    tca = blk * 2 + tb
                    r = e.tensor_copy(
                        s0T[:, tca * 1024 + g8 * 128: tca * 1024 + (g8 + 1) * 128],
                        ps_t[:, i * 128:(i + 1) * 128])
                return r

            c.step(("vector", evt))

    def s0store(e, c):
        for tc in range(8):
            e.dma_start(out=c.d["s0"][tc * 128:(tc + 1) * 128, :],
                        in_=s0T[:, tc * 1024:(tc + 1) * 1024]).then_inc(c.S("so"), 16)
            c.bump("so", 16)
        wdma(e, c, "so")
        return _mark(e)

    c.step(("gpsimd", s0store))
    c.free("g2_qt0", "g2_qt1", "g2_qt2", "g2_kt0", "g2_kt1", "g2_kt2",
           "g2_vt0", "g2_vt1", "g2_vt2",
           "g2_ex0", "g2_ex1", "g2_crow", "g2_cbc", "g2_r32", "g2_ot",
           "g2_id", "g2_ones", "g2_pss0", "g2_pss1", "g2_pso0", "g2_pso1",
           "g2_pst")


def build_g3(c):
    nc = c.nc
    if not FUSED:
        load_xnT(c, "xn_bf")
    xnT = c.t["xnT"]

    wt = [c.sb(f"g3_wt{i}", [128, 4096], BF16) for i in range(4)]
    lo = c.sb("g3_lo", [16, 2 * T], BF16)
    a1 = c.sb("g3_a1", [128, NEC * 16], BF16)
    a3 = c.sb("g3_a3", [128, NEC * 16], BF16)
    b1 = c.sb("g3_b1", [16, HID], BF16)
    b3 = c.sb("g3_b3", [16, HID], BF16)
    h1s = [c.sb(f"g3_h1s{i}", [128, T], BF16) for i in range(2)]
    h3s = [c.sb(f"g3_h3s{i}", [128, T], BF16) for i in range(2)]
    gsb = [c.sb(f"g3_g{i}", [128, T], BF16) for i in range(2)]
    ps1 = [c.ps(f"g3_ps1{i}", [128, T]) for i in range(2)]
    ps3 = [c.ps(f"g3_ps3{i}", [128, T]) for i in range(2)]

    def prolog(e, c):
        e.dma_start(out=a1[:].rearrange("p (ec r) -> p ec r", r=16),
                    in_=c.d["A1T"].rearrange("(ec p) r -> p ec r", p=128)
                    ).then_inc(c.S("pl"), 16)
        e.dma_start(out=a3[:].rearrange("p (ec r) -> p ec r", r=16),
                    in_=c.d["A3T"].rearrange("(ec p) r -> p ec r", p=128)
                    ).then_inc(c.S("pl"), 16)
        e.dma_start(out=b1[:], in_=c.d["B1T"]).then_inc(c.S("pl"), 16)
        e.dma_start(out=b3[:], in_=c.d["B3T"]).then_inc(c.S("pl"), 16)
        c.bump("pl", 64)
        return _mark(e)

    def wload(e, c, hc):
        b = hc % 2
        par = "w" + str(b)
        e.dma_start(out=wt[2 * b][:].rearrange("p (ec col) -> p ec col", col=128),
                    in_=c.d["w1c"][hc].rearrange("(ec p) col -> p ec col", p=128)
                    ).then_inc(c.S(par), 16)
        e.dma_start(out=wt[2 * b + 1][:].rearrange("p (ec col) -> p ec col", col=128),
                    in_=c.d["w3c"][hc].rearrange("(ec p) col -> p ec col", p=128)
                    ).then_inc(c.S(par), 16)
        c.bump(par, 32)
        return _mark(e)

    c.step(("gpsimd", prolog))
    c.step(("gpsimd", lambda e, c: wload(e, c, 0)))

    def lomm(e, c):
        wdma(e, c, "pl")
        r = None
        for i, a in enumerate((a1, a3)):
            ps = ps1[i]
            for th in range(2):
                for ec in range(NEC):
                    r = e.matmul(ps[0:16, th * 512:(th + 1) * 512],
                                 a[:, ec * 16:(ec + 1) * 16],
                                 xnT[:, ec * T + th * 512: ec * T + (th + 1) * 512],
                                 start=(ec == 0), stop=(ec == NEC - 1))
        return r

    c.step(("tensor", lomm))

    def loev(e, c):
        VG(e, c, e.tensor_scalar(lo[:, 0:T], ps1[0][0:16, :], 2.0, None, ALU.mult))
        VG(e, c, e.tensor_scalar(lo[:, T:2 * T], ps1[1][0:16, :], 2.0, None, ALU.mult))
        return None

    c.step(("vector", loev))

    def mm_hc(e, c, hc):
        b = hc % 2
        wdma(e, c, "w" + str(b))
        r = None
        for which in range(2):
            wti = wt[2 * b + which]
            bx = b1 if which == 0 else b3
            ps = (ps1 if which == 0 else ps3)[b]
            for th in range(2):
                for ec in range(NEC):
                    e.matmul(ps[:, th * 512:(th + 1) * 512],
                             wti[:, ec * 128:(ec + 1) * 128],
                             xnT[:, ec * T + th * 512: ec * T + (th + 1) * 512],
                             start=(ec == 0), stop=False)
                r = e.matmul(ps[:, th * 512:(th + 1) * 512],
                             bx[:, hc * 128:(hc + 1) * 128],
                             lo[:, which * T + th * 512: which * T + (th + 1) * 512],
                             start=False, stop=True)
        return r

    def evacs(e, c, hc):
        return e.activation(h1s[hc % 2][:], ps1[hc % 2][:], AF.Silu)

    def evacv(e, c, hc):
        return e.tensor_copy(h3s[hc % 2][:], ps3[hc % 2][:])

    def gmult(e, c, hc):
        wdma(e, c, "s" + str(hc % 2))
        return e.tensor_tensor(gsb[hc % 2][:], h1s[hc % 2][:], h3s[hc % 2][:],
                               ALU.mult)

    def gstore(e, c, hc):
        par = "s" + str(hc % 2)
        e.dma_start(out=c.d["gdram"][hc * 128:(hc + 1) * 128, :],
                    in_=gsb[hc % 2][:]).then_inc(c.S(par), 16)
        c.bump(par, 16)
        return _mark(e)

    for i in range(NHC + 3):
        parts = []
        if i < NHC:
            parts.append(("tensor", lambda e, c, i=i: mm_hc(e, c, i)))
        if 0 <= i - 1 < NHC:
            parts.append(("scalar", lambda e, c, i=i: evacs(e, c, i - 1)))
            parts.append(("vector", lambda e, c, i=i: evacv(e, c, i - 1)))
        if 0 <= i - 2 < NHC:
            parts.append(("vector", lambda e, c, i=i: gmult(e, c, i - 2)))
        if 0 <= i - 3 < NHC:
            parts.append(("gpsimd", lambda e, c, i=i: gstore(e, c, i - 3)))
        if 0 <= i < NHC and i + 1 < NHC:
            parts.append(("gpsimd", lambda e, c, i=i: wload(e, c, i + 1)))
        c.step(*parts)

    def fin(e, c):
        wdma(e, c, "s0")
        wdma(e, c, "s1")
        return _mark(e)

    c.step(("gpsimd", fin))
    c.free("g3_wt0", "g3_wt1", "g3_wt2", "g3_wt3", "g3_lo", "g3_a1", "g3_a3",
           "g3_b1", "g3_b3", "g3_h1s0", "g3_h1s1", "g3_h3s0", "g3_h3s1",
           "g3_g0", "g3_g1", "g3_ps10", "g3_ps11", "g3_ps30", "g3_ps31")


def build_g4(c):
    nc = c.nc
    s0T = c.t["s0T"]

    if not FUSED:
        def s0load(e, c):
            for tc in range(8):
                e.dma_start(out=s0T[:, tc * 1024:(tc + 1) * 1024],
                            in_=c.d["s0"][tc * 128:(tc + 1) * 128, :]
                            ).then_inc(c.S("pl"), 16)
                c.bump("pl", 16)
            wdma(e, c, "pl")
            return _mark(e)

        c.step(("gpsimd", s0load))

    gt = [c.sb(f"g4_gt{i}", [128, T], BF16) for i in range(2)]
    a2 = c.sb("g4_a2", [128, NHC * 16], BF16)
    lo2 = c.sb("g4_lo2", [16, T], BF16)
    ps_lo = c.ps("g4_plo", [16, T])

    def prolog(e, c):
        e.dma_start(out=a2[:].rearrange("p (hc r) -> p hc r", r=16),
                    in_=c.d["A2T"].rearrange("(hc p) r -> p hc r", p=128)
                    ).then_inc(c.S("pl"), 16)
        c.bump("pl", 16)
        return _mark(e)

    def gload(e, c, hc, slot):
        par = "a" + str(slot)
        e.dma_start(out=gt[slot][:],
                    in_=c.d["gdram"][hc * 128:(hc + 1) * 128, :]).then_inc(c.S(par), 16)
        c.bump(par, 16)
        return _mark(e)

    c.step(("gpsimd", prolog))
    c.step(("gpsimd", lambda e, c: gload(e, c, 0, 0)))
    for hc in range(NHC):
        def mm(e, c, hc=hc):
            if hc == 0:
                wdma(e, c, "pl")
            wdma(e, c, "a" + str(hc % 2))
            e.matmul(ps_lo[0:16, 0:512], a2[:, hc * 16:(hc + 1) * 16],
                     gt[hc % 2][:, 0:512], start=(hc == 0), stop=(hc == NHC - 1))
            return e.matmul(ps_lo[0:16, 512:1024], a2[:, hc * 16:(hc + 1) * 16],
                            gt[hc % 2][:, 512:1024], start=(hc == 0),
                            stop=(hc == NHC - 1))

        parts = [("tensor", mm)]
        if hc + 1 < NHC:
            parts.append(("gpsimd", lambda e, c, hc=hc: gload(e, c, hc + 1, (hc + 1) % 2)))
        c.step(*parts)

    c.step(("vector", lambda e, c: VG(e, c, e.tensor_scalar(lo2[:], ps_lo[0:16, :],
                                                             2.0, None, ALU.mult)) and None))

    def lostore(e, c):
        e.dma_start(out=c.d["h2lo_d"], in_=lo2[:]).then_inc(c.S("so"), 16)
        c.bump("so", 16)
        return _mark(e)

    c.step(("gpsimd", lostore))
    c.free("g4_plo")

    w2t = [c.sb(f"g4_w2t{i}", [128, 512], BF16) for i in range(3)]
    gt3 = [c.sb(f"g4_gt3{i}", [128, T], BF16) for i in range(3)]
    b2t = c.sb("g4_b2t", [16, EMB], BF16)
    silu_t = c.sb("g4_silu", [128, 512], BF16)
    sg = c.sb("g4_sg", [128, 512], F32)
    xts = [c.sb(f"g4_xt{i}", [128, 512], F32) for i in range(2)]
    xnew_t = [c.sb(f"g4_xnew{i}", [128, 512], F32) for i in range(2)]
    ps8 = [c.ps(f"g4_ps{i}", [128, 512]) for i in range(8)]

    def prolog2(e, c):
        e.dma_start(out=b2t[:], in_=c.d["B2T"]).then_inc(c.S("pl"), 16)
        c.bump("pl", 16)
        return _mark(e)

    def wgload(e, c, i):
        es, hc = i // NHC, i % NHC
        slot = i % 3
        par = "w" + str(i % 2)
        e.dma_start(out=w2t[slot][:],
                    in_=c.d["w2s"][es, hc * 128:(hc + 1) * 128, :]
                    ).then_inc(c.S(par), 16)
        e.dma_start(out=gt3[slot][:],
                    in_=c.d["gdram"][hc * 128:(hc + 1) * 128, :]).then_inc(c.S(par), 16)
        c.bump(par, 32)
        return _mark(e)

    c.step(("gpsimd", prolog2))
    c.step(("gpsimd", lambda e, c: wgload(e, c, 0)))
    c.step(("gpsimd", lambda e, c: wgload(e, c, 1)))
    for es in range(8):
        for hc in range(NHC):
            i = es * NHC + hc

            def mm(e, c, i=i, es=es, hc=hc):
                if i == 0:
                    wdma(e, c, "pl")
                slot = i % 3
                wdma(e, c, "w" + str(i % 2))
                r = None
                for tc in range(8):
                    r = e.matmul(ps8[tc][:], gt3[slot][:, tc * 128:(tc + 1) * 128],
                                 w2t[slot][:], start=(hc == 0), stop=False)
                return r

            parts = [("tensor", mm)]
            if i + 2 < 8 * NHC:
                parts.append(("gpsimd", lambda e, c, i=i: wgload(e, c, i + 2)))
            c.step(*parts)

        def b2mm(e, c, es=es):
            r = None
            for tc in range(8):
                r = e.matmul(ps8[tc][:], lo2[:, tc * 128:(tc + 1) * 128],
                             b2t[:, es * 512:(es + 1) * 512], start=False, stop=True)
            return r

        c.step(("tensor", b2mm))

        def xload(e, c, es, tc):
            par = "x" + str(tc % 2)
            e.dma_start(out=xts[tc % 2][:],
                        in_=c.d["x"][tc * 128:(tc + 1) * 128,
                                     es * 512:(es + 1) * 512]).then_inc(c.S(par), 16)
            c.bump(par, 16)
            return _mark(e)

        c.step(("sync", lambda e, c, es=es: xload(e, c, es, 0)))
        c.step(("sync", lambda e, c, es=es: xload(e, c, es, 1)))
        for tc in range(8):
            sparts = [("scalar", lambda e, c, tc=tc:
                       e.activation(silu_t[:], ps8[tc][:], AF.Silu))]
            if 2 <= tc + 1 < 8:
                sparts.append(("sync", lambda e, c, es=es, tc=tc:
                               xload(e, c, es, tc + 1)))
            c.step(*sparts)

            def ep(e, c, es=es, tc=tc):
                slot = tc % 2
                wdma(e, c, "x" + str(slot))
                sap = (s0T[:, tc * 1024:(tc + 1) * 1024]
                       .rearrange("p (g d) -> p g d", g=8)
                       [:, :, es * 16:(es + 1) * 16]
                       .transpose([0, 2, 1])
                       .rearrange("p d (g o) -> p d g o", o=1)
                       .broadcast_to((128, 16, 8, 4)))
                e.tensor_tensor(sg[:].rearrange("p (d g j) -> p d g j", d=16, g=8),
                                silu_t[:].rearrange("p (d g j) -> p d g j", d=16, g=8),
                                sap, ALU.mult)
                wdma(e, c, "s" + str(slot))
                return e.tensor_tensor(xnew_t[slot][:], xts[slot][:], sg[:], ALU.add)

            c.step(("vector", ep))

            def nstore(e, c, es=es, tc=tc):
                slot = tc % 2
                par = "s" + str(slot)
                e.dma_start(out=c.d["xnew_d"][tc * 128:(tc + 1) * 128,
                                              es * 512:(es + 1) * 512],
                            in_=xnew_t[slot][:]).then_inc(c.S(par), 16)
                c.bump(par, 16)
                return _mark(e)

            c.step(("gpsimd", nstore))

    def fin(e, c):
        wdma(e, c, "s0")
        wdma(e, c, "s1")
        wdma(e, c, "so")
        return _mark(e)

    c.step(("gpsimd", fin))
    c.free("g4_gt0", "g4_gt1", "g4_a2", "g4_lo2", "g4_w2t0", "g4_w2t1", "g4_w2t2",
           "g4_gt30", "g4_gt31", "g4_gt32",
           "g4_b2t", "g4_silu", "g4_sg", "g4_xt0", "g4_xt1", "g4_xnew0",
           "g4_xnew1", *(f"g4_ps{i}" for i in range(8)))
    c.free("s0T")
    del c.t["s0T"]


def build_g5(c):
    nc = c.nc
    ln_global(c, "xnew_d", "xn1_bf", "l1")
    load_xnT(c, "xn1_bf")
    xnT = c.t["xnT"]

    ident = c.sb("g5_id", [128, 128], BF16)
    identF = c.sb("g5_idF", [128, 128], F32)
    onesc = c.sb("g5_onesc", [128, 1], BF16)
    onesr = c.sb("g5_onesr", [1, 128], BF16)
    is2 = c.sb("g5_is2", [128, 1], F32)
    isA = c.sb("g5_isA", [128, 1], F32)
    boutr = c.sb("g5_boutr", [1, EMB], BF16)
    id_d = nc.inline_tensor(np.eye(128, dtype=BF), name="g5ident").ap()
    idF_d = nc.inline_tensor(np.eye(128, dtype=np.float32), name="g5identF").ap()
    oc_d = nc.inline_tensor(np.ones((128, 1), dtype=BF), name="g5onesc").ap()
    or_d = nc.inline_tensor(np.ones((1, 128), dtype=BF), name="g5onesr").ap()

    pnames = ["ln1_w", "ln1_b", "phz_re", "phz_im", "lc_re", "lc_im",
              "sc_lin", "ln2_w", "ln2_b", "b2", "bin", "lnl_w", "lnl_b", "b1"]
    prows = {"b2": 16, "bin": 4, "lnl_w": 4, "lnl_b": 4, "b1": 64}
    pstage = c.sb("g5_pstage", [128, 128], F32)
    ppar = {nm: c.sb("g5_pp_" + nm, [128, prows.get(nm, 16)], F32) for nm in pnames}
    ps_pt = c.ps("g5_pspt", [128, 128])

    def prolog(e, c):
        for src, dst in ((id_d, ident), (idF_d, identF), (oc_d, onesc),
                         (or_d, onesr), (c.d["is2"], is2),
                         (c.d["bout_row"], boutr)):
            e.dma_start(out=dst[:], in_=src).then_inc(c.S("pl"), 16)
            c.bump("pl", 16)
        return _mark(e)

    c.step(("gpsimd", prolog))

    def isAset(e, c):
        wdma(e, c, "pl")
        VG(e, c, e.memset(isA[:], 1.0))
        VG(e, c, e.tensor_tensor(isA[:], isA[:], is2[:], ALU.subtract))
        return None

    c.step(("vector", isAset))

    for nm in pnames:
        rows = prows.get(nm, 16)

        def pload(e, c, nm=nm, rows=rows):
            e.dma_start(out=pstage[0:rows, :], in_=c.d["p_" + nm]).then_inc(c.S("pl"), 16)
            c.bump("pl", 16)
            return _mark(e)

        c.step(("gpsimd", pload))

        def ptp(e, c, rows=rows):
            wdma(e, c, "pl")
            return e.transpose(ps_pt[:, 0:rows], pstage[0:rows, :],
                               identF[0:rows, 0:rows])

        c.step(("tensor", ptp))
        c.step(("vector", lambda e, c, nm=nm, rows=rows:
                VG(e, c, e.tensor_copy(ppar[nm][:, 0:rows], ps_pt[:, 0:rows])) and None))

    h = c.sb("g5_h", [128, NDC * T], F32)
    winb = [c.sb(f"g5_winb{i}", [128, 4096], BF16) for i in range(2)]
    ps_a = c.ps("g5_psa", [128, 512])
    ps_b = c.ps("g5_psb", [128, 512])

    def winload(e, c, dc):
        par = "w" + str(dc % 2)
        e.dma_start(out=winb[dc % 2][:].rearrange("p (ec col) -> p ec col", col=128),
                    in_=c.d["WinT"].rearrange("(ec p) d -> p ec d", p=128)
                    [:, :, dc * 128:(dc + 1) * 128]).then_inc(c.S(par), 16)
        c.bump(par, 16)
        return _mark(e)

    c.step(("gpsimd", lambda e, c: winload(e, c, 0)))
    for dc in range(NDC):
        def wmm(e, c, dc=dc):
            wdma(e, c, "w" + str(dc % 2))
            r = None
            for th in range(2):
                ps = ps_a if th == 0 else ps_b
                for ec in range(NEC):
                    r = e.matmul(ps[:], winb[dc % 2][:, ec * 128:(ec + 1) * 128],
                                 xnT[:, ec * T + th * 512: ec * T + (th + 1) * 512],
                                 start=(ec == 0), stop=(ec == NEC - 1))
            return r

        parts = [("tensor", wmm)]
        if dc + 1 < NDC:
            parts.append(("gpsimd", lambda e, c, dc=dc: winload(e, c, dc + 1)))
        c.step(*parts)

        def wev(e, c, dc=dc):
            e.tensor_scalar(h[:, dc * T: dc * T + 512], ps_a[:],
                            ppar["bin"][:, dc:dc + 1], None, ALU.add)
            return e.tensor_scalar(h[:, dc * T + 512: dc * T + 1024], ps_b[:],
                                   ppar["bin"][:, dc:dc + 1], None, ALU.add)

        c.step(("vector", wev))

    def hdump(slot):
        if not DEBUG_G5:
            return

        def d(e, c, slot=slot):
            e.dma_start(out=c.d["h_dbg"][:, slot * NDC * T:(slot + 1) * NDC * T],
                        in_=h[:]).then_inc(c.S("hdbg"), 16)
            c.bump("hdbg", 16)
            wdma(e, c, "hdbg")
            return _mark(e)

        c.step(("gpsimd", d))

    hdump(0)
    c.free("g5_winb0", "g5_winb1")
    c.free("xnT")
    del c.t["xnT"]

    hb = c.sb("g5_hb", [128, NDC * T], BF16)
    urho = c.sb("g5_urho", [128, 28], F32)
    dini = c.sb("g5_dini", [128, 16], F32)
    tails = c.sb("g5_tails", [128, 8], F32)
    recv = c.sb("g5_recv", [128, 8], F32)
    row = c.sb("g5_row", [1, 4 * T], F32)
    rowb = c.sb("g5_rowb", [1, 2 * T], BF16)
    zero1 = c.sb("g5_zero", [128, 1], F32)
    hsq = c.sb("g5_hsq", [128, 512], BF16)
    yv = c.sb("g5_yv", [128, T], F32)
    Tcb = c.sb("g5_Tc", [128, NDC * T], BF16)
    Tsb = c.sb("g5_Ts", [128, NDC * T], BF16)
    Tf32c = c.sb("g5_Tfc", [128, T], F32)
    Tf32s = c.sb("g5_Tfs", [128, T], F32)
    ytr = c.sb("g5_ytr", [128, T], F32)
    yti = c.sb("g5_yti", [128, T], F32)
    dr = c.sb("g5_dr", [128, NDC * T], BF16)
    di = c.sb("g5_di", [128, NDC * T], BF16)
    yh = c.sb("g5_yh", [128, NFC * T], BF16)
    w1t = [c.sb(f"g5_w1t{i}", [128, 512], BF16) for i in range(3)]
    w2t = [c.sb(f"g5_w2t{i}", [128, 2048], BF16) for i in range(2)]
    ps_bc1 = c.ps("g5_psbc1", [128, T])
    ps_bc2 = c.ps("g5_psbc2", [128, T])
    cc_in = [nc.dram_tensor(f"cc_in{l}", [128, 8], F32) for l in range(DEPTH1)]
    cc_out = [nc.dram_tensor(f"cc_out{l}", [128, 8], F32) for l in range(DEPTH1)]

    c.step(("vector", lambda e, c: e.memset(zero1[:], 0.0)))

    def feat_stats(c):
        c.step(("vector", lambda e, c: e.tensor_copy(hb[:], h[:])))

        def summ(e, c):
            r = None
            for th in range(2):
                ps = ps_a if th == 0 else ps_b
                for dc in range(NDC):
                    r = e.matmul(ps[0:1, :], onesc[:],
                                 hb[:, dc * T + th * 512: dc * T + (th + 1) * 512],
                                 start=(dc == 0), stop=(dc == NDC - 1))
            return r

        c.step(("tensor", summ))

        def sev(e, c):
            VG(e, c, e.tensor_copy(row[:, 0:512], ps_a[0:1, :]))
            VG(e, c, e.tensor_copy(row[:, 512:1024], ps_b[0:1, :]))
            return None

        c.step(("vector", sev))
        for th in range(2):
            for dc in range(NDC):
                c.step(("scalar", lambda e, c, th=th, dc=dc:
                        e.activation(hsq[:],
                                     hb[:, dc * T + th * 512: dc * T + (th + 1) * 512],
                                     AF.Square)))
                c.step(("tensor", lambda e, c, th=th, dc=dc:
                        e.matmul((ps_a if th == 0 else ps_b)[0:1, :], onesc[:],
                                 hsq[:, 0:512], start=(dc == 0),
                                 stop=(dc == NDC - 1))))

        def stats(e, c):
            VG(e, c, e.tensor_copy(row[:, T:T + 512], ps_a[0:1, :]))
            VG(e, c, e.tensor_copy(row[:, T + 512:2 * T], ps_b[0:1, :]))
            VG(e, c, e.tensor_scalar(row[:, 2 * T:3 * T], row[:, 0:T], -1.0 / D1,
                                     None, ALU.mult))
            VG(e, c, e.tensor_scalar(row[:, T:2 * T], row[:, T:2 * T], 1.0 / D1,
                                     None, ALU.mult))
            VG(e, c, e.tensor_tensor(row[:, 3 * T:4 * T], row[:, 2 * T:3 * T],
                                     row[:, 2 * T:3 * T], ALU.mult))
            VG(e, c, e.tensor_tensor(row[:, T:2 * T], row[:, T:2 * T],
                                     row[:, 3 * T:4 * T], ALU.subtract))
            VG(e, c, e.tensor_scalar(row[:, T:2 * T], row[:, T:2 * T], EPS,
                                     None, ALU.add))
            return None

        c.step(("vector", stats))
        c.step(("scalar", lambda e, c: VG(e, c, e.activation(row[:, 3 * T:4 * T],
                                                              row[:, T:2 * T], AF.Sqrt)) and None))

        def rstd(e, c):
            VG(e, c, e.reciprocal(row[:, T:2 * T], row[:, 3 * T:4 * T]), extra=True)
            VG(e, c, e.tensor_copy(rowb[:, 0:T], row[:, 2 * T:3 * T]))
            VG(e, c, e.tensor_copy(rowb[:, T:2 * T], row[:, T:2 * T]))
            return None

        c.step(("vector", rstd))

        def bcast(e, c):
            r = None
            for th in range(2):
                e.matmul(ps_bc1[:, th * 512:(th + 1) * 512], onesr[:],
                         rowb[:, th * 512:(th + 1) * 512], start=True, stop=True)
                r = e.matmul(ps_bc2[:, th * 512:(th + 1) * 512], onesr[:],
                             rowb[:, T + th * 512:T + (th + 1) * 512],
                             start=True, stop=True)
            return r

        c.step(("tensor", bcast))

    def sdump(slot, tile, l, dc, width=T):
        if not DEBUG_G5 or l != 0 or dc != 0:
            return

        def d(e, c, slot=slot, tile=tile, width=width):
            e.dma_start(out=c.d["s_dbg"][:, slot * T:slot * T + width],
                        in_=tile).then_inc(c.S("hdbg"), 16)
            c.bump("hdbg", 16)
            wdma(e, c, "hdbg")
            return _mark(e)

        c.step(("gpsimd", d))

    for l in range(DEPTH1):
        feat_stats(c)

        def uprep(e, c, l=l):
            pr = ppar["phz_re"][:, l * 4:(l + 1) * 4]
            pi = ppar["phz_im"][:, l * 4:(l + 1) * 4]
            VG(e, c, e.tensor_tensor(urho[:, 0:4], pr, pr, ALU.mult))
            VG(e, c, e.tensor_tensor(urho[:, 4:8], pi, pi, ALU.mult))
            VG(e, c, e.tensor_tensor(urho[:, 0:4], urho[:, 0:4], urho[:, 4:8], ALU.add))
            return None

        c.step(("vector", uprep))
        c.step(("scalar", lambda e, c: VG(e, c, e.activation(urho[:, 4:8],
                                                              urho[:, 0:4], AF.Sqrt)) and None))

        def u2(e, c, l=l):
            pr = ppar["phz_re"][:, l * 4:(l + 1) * 4]
            pi = ppar["phz_im"][:, l * 4:(l + 1) * 4]
            VG(e, c, e.reciprocal(urho[:, 8:12], urho[:, 4:8]), extra=True)
            VG(e, c, e.tensor_tensor(urho[:, 12:16], pr, urho[:, 8:12], ALU.mult))
            VG(e, c, e.tensor_tensor(urho[:, 16:20], pi, urho[:, 8:12], ALU.mult))
            return None

        c.step(("vector", u2))
        c.step(("scalar", lambda e, c: VG(e, c, e.activation(urho[:, 20:24],
                                                             urho[:, 4:8], AF.Tanh)) and None))

        def dini_intra(e, c, l=l):
            lcr = ppar["lc_re"][:, l * 4:(l + 1) * 4]
            lci = ppar["lc_im"][:, l * 4:(l + 1) * 4]
            ur, ui = urho[:, 12:16], urho[:, 16:20]
            VG(e, c, e.tensor_tensor(dini[:, 0:4], lcr, ur, ALU.mult))
            VG(e, c, e.tensor_tensor(urho[:, 24:28], lci, ui, ALU.mult))
            VG(e, c, e.tensor_tensor(dini[:, 0:4], dini[:, 0:4], urho[:, 24:28],
                                     ALU.subtract))
            VG(e, c, e.tensor_tensor(dini[:, 4:8], lcr, ui, ALU.mult))
            VG(e, c, e.tensor_tensor(urho[:, 24:28], lci, ur, ALU.mult))
            VG(e, c, e.tensor_tensor(dini[:, 4:8], dini[:, 4:8], urho[:, 24:28],
                                     ALU.add))
            VG(e, c, e.tensor_tensor(dini[:, 0:4], dini[:, 0:4],
                                     isA[:, 0:1].broadcast_to((128, 4)), ALU.mult))
            VG(e, c, e.tensor_tensor(dini[:, 4:8], dini[:, 4:8],
                                     isA[:, 0:1].broadcast_to((128, 4)), ALU.mult))
            return None

        c.step(("vector", dini_intra))

        for dc in range(NDC):
            def ybuild(e, c, l=l, dc=dc):
                wcol = ppar["ln1_w"][:, l * 4 + dc:l * 4 + dc + 1]
                bcol = ppar["ln1_b"][:, l * 4 + dc:l * 4 + dc + 1]
                e.tensor_tensor(yv[:], h[:, dc * T:(dc + 1) * T], ps_bc1[:], ALU.add)
                e.tensor_tensor(yv[:], yv[:], ps_bc2[:], ALU.mult)
                return e.tensor_scalar(yv[:], yv[:], wcol, bcol, ALU.mult, ALU.add)

            c.step(("vector", ybuild))
            c.step(("scalar", lambda e, c: e.activation(yv[:], yv[:], AF.Silu)))
            sdump(0, yv[:], l, dc)

            def tbl0(e, c, dc=dc):
                VG(e, c, e.memset(Tf32c[:, 0:1], 1.0))
                VG(e, c, e.memset(Tf32s[:, 0:1], 0.0))
                VG(e, c, e.tensor_copy(urho[:, 24:25], urho[:, 12 + dc:13 + dc]))
                VG(e, c, e.tensor_copy(urho[:, 25:26], urho[:, 16 + dc:17 + dc]))
                return None

            c.step(("vector", tbl0))
            for kk in range(10):
                n = 1 << kk

                def dbl(e, c, dc=dc, kk=kk, n=n):
                    cn = urho[:, 24:25]
                    sn = urho[:, 25:26]
                    A = Tf32c[:, 0:n]
                    B = Tf32s[:, 0:n]
                    VG(e, c, e.tensor_scalar(ytr[:, 0:n], B, sn, None, ALU.mult))
                    VG(e, c, e.tensor_scalar(yti[:, 0:n], A, sn, None, ALU.mult))
                    VG(e, c, e.scalar_tensor_tensor(Tf32c[:, n:2 * n], A, cn,
                                                    ytr[:, 0:n], ALU.mult,
                                                    ALU.subtract))
                    VG(e, c, e.scalar_tensor_tensor(Tf32s[:, n:2 * n], B, cn,
                                                    yti[:, 0:n], ALU.mult, ALU.add))
                    if kk < 9:
                        VG(e, c, e.tensor_tensor(urho[:, 26:27], cn, cn, ALU.mult))
                        VG(e, c, e.tensor_tensor(urho[:, 27:28], sn, sn, ALU.mult))
                        VG(e, c, e.tensor_tensor(urho[:, 27:28], urho[:, 26:27],
                                                 urho[:, 27:28], ALU.subtract))
                        VG(e, c, e.tensor_tensor(urho[:, 26:27], cn, sn, ALU.mult))
                        VG(e, c, e.tensor_copy(urho[:, 24:25], urho[:, 27:28]))
                        VG(e, c, e.tensor_scalar(urho[:, 25:26], urho[:, 26:27],
                                                 2.0, None, ALU.mult))
                    return None

                c.step(("vector", dbl))

            def tblcast(e, c, dc=dc):
                base = dc * T
                e.tensor_copy(Tcb[:, base:base + T], Tf32c[:])
                return e.tensor_copy(Tsb[:, base:base + T], Tf32s[:])

            c.step(("vector", tblcast))
            sdump(1, Tf32c[:], l, dc)
            sdump(2, Tf32s[:], l, dc)

            def ytilde(e, c, dc=dc):
                e.tensor_tensor(ytr[:], yv[:], Tf32c[:], ALU.mult)
                return e.scalar_tensor_tensor(yti[:], yv[:], -1.0,
                                              Tf32s[:], ALU.mult, ALU.mult)

            c.step(("vector", ytilde))

            def scans(e, c, dc=dc):
                base = dc * T
                rho = urho[:, 20 + dc:21 + dc].broadcast_to((128, T))
                VG(e, c, e.tensor_tensor_scan(dr[:, base:base + T], rho, ytr[:],
                                              dini[:, dc:dc + 1], ALU.mult, ALU.add),
                   extra=True)
                VG(e, c, e.tensor_tensor_scan(di[:, base:base + T], rho, yti[:],
                                              dini[:, 4 + dc:5 + dc], ALU.mult,
                                              ALU.add), extra=True)
                return None

            c.step(("vector", scans))
            sdump(3, dr[:, 0:T], l, dc)
            sdump(4, di[:, 0:T], l, dc)

            def tail(e, c, dc=dc):
                base = dc * T
                VG(e, c, e.tensor_tensor(tails[:, dc:dc + 1],
                                         dr[:, base + T - 1:base + T],
                                         Tcb[:, base + T - 1:base + T], ALU.mult))
                VG(e, c, e.tensor_tensor(urho[:, 24:25],
                                         di[:, base + T - 1:base + T],
                                         Tsb[:, base + T - 1:base + T], ALU.mult))
                VG(e, c, e.tensor_tensor(tails[:, dc:dc + 1], tails[:, dc:dc + 1],
                                         urho[:, 24:25], ALU.subtract))
                VG(e, c, e.tensor_tensor(tails[:, 4 + dc:5 + dc],
                                         dr[:, base + T - 1:base + T],
                                         Tsb[:, base + T - 1:base + T], ALU.mult))
                VG(e, c, e.tensor_tensor(urho[:, 24:25],
                                         di[:, base + T - 1:base + T],
                                         Tcb[:, base + T - 1:base + T], ALU.mult))
                VG(e, c, e.tensor_tensor(tails[:, 4 + dc:5 + dc],
                                         tails[:, 4 + dc:5 + dc],
                                         urho[:, 24:25], ALU.add))
                return None

            c.step(("vector", tail))

        def ccgo(e, c, l=l):
            e.dma_start(out=cc_in[l].ap(), in_=tails[:]).then_inc(c.S("cc"), 16)
            c.bump("cc", 16)
            wdma(e, c, "cc")
            e.collective_compute("AllReduce", ALU.add,
                                 replica_groups=[[0, 1], [2, 3], [4, 5], [6, 7]],
                                 ins=[cc_in[l].ap().opt()],
                                 outs=[cc_out[l].ap().opt()]).then_inc(c.S("cd"), 1)
            c.bump("cd", 1)
            wdma(e, c, "cd")
            e.dma_start(out=recv[:], in_=cc_out[l].ap()).then_inc(c.S("cr"), 16)
            c.bump("cr", 16)
            wdma(e, c, "cr")
            return _mark(e)

        c.step(("gpsimd", ccgo))

        def dini_corr(e, c, l=l):
            ur, ui = urho[:, 12:16], urho[:, 16:20]
            VG(e, c, e.tensor_tensor(recv[:], recv[:], tails[:], ALU.subtract))
            VG(e, c, e.tensor_tensor(dini[:, 8:12], recv[:, 0:4], ur, ALU.mult))
            VG(e, c, e.tensor_tensor(urho[:, 24:28], recv[:, 4:8], ui, ALU.mult))
            VG(e, c, e.tensor_tensor(dini[:, 8:12], dini[:, 8:12], urho[:, 24:28],
                                     ALU.subtract))
            VG(e, c, e.tensor_tensor(dini[:, 12:16], recv[:, 0:4], ui, ALU.mult))
            VG(e, c, e.tensor_tensor(urho[:, 24:28], recv[:, 4:8], ur, ALU.mult))
            VG(e, c, e.tensor_tensor(dini[:, 12:16], dini[:, 12:16], urho[:, 24:28],
                                     ALU.add))
            VG(e, c, e.tensor_tensor(dini[:, 8:12], dini[:, 8:12],
                                     is2[:, 0:1].broadcast_to((128, 4)), ALU.mult))
            VG(e, c, e.tensor_tensor(dini[:, 12:16], dini[:, 12:16],
                                     is2[:, 0:1].broadcast_to((128, 4)), ALU.mult))
            return None

        c.step(("vector", dini_corr))

        for dc in range(NDC):
            def corr(e, c, dc=dc):
                base = dc * T
                rho = urho[:, 20 + dc:21 + dc].broadcast_to((128, T))
                z = zero1[:, 0:1].broadcast_to((128, T))
                VG(e, c, e.tensor_tensor_scan(ytr[:], rho, z, dini[:, 8 + dc:9 + dc],
                                              ALU.mult, ALU.add), extra=True)
                VG(e, c, e.tensor_tensor_scan(yti[:], rho, z, dini[:, 12 + dc:13 + dc],
                                              ALU.mult, ALU.add), extra=True)
                e.tensor_tensor(ytr[:], ytr[:], dr[:, base:base + T], ALU.add)
                return e.tensor_tensor(yti[:], yti[:], di[:, base:base + T], ALU.add)

            c.step(("vector", corr))

            def comb(e, c, l=l, dc=dc):
                base = dc * T
                scl = ppar["sc_lin"][:, l * 4 + dc:l * 4 + dc + 1]
                e.tensor_tensor(ytr[:], ytr[:], Tcb[:, base:base + T], ALU.mult)
                e.tensor_tensor(yti[:], yti[:], Tsb[:, base:base + T], ALU.mult)
                e.tensor_tensor(ytr[:], ytr[:], yti[:], ALU.subtract)
                return e.scalar_tensor_tensor(h[:, base:base + T], ytr[:], scl,
                                              h[:, base:base + T], ALU.mult, ALU.add)

            c.step(("vector", comb))
            sdump(5, ytr[:], l, dc)

        hdump(1 + 2 * l)
        feat_stats(c)
        for dc in range(NDC):
            def yb2(e, c, l=l, dc=dc):
                wcol = ppar["ln2_w"][:, l * 4 + dc:l * 4 + dc + 1]
                bcol = ppar["ln2_b"][:, l * 4 + dc:l * 4 + dc + 1]
                e.tensor_tensor(yv[:], h[:, dc * T:(dc + 1) * T], ps_bc1[:], ALU.add)
                e.tensor_tensor(yv[:], yv[:], ps_bc2[:], ALU.mult)
                return e.tensor_scalar(hb[:, dc * T:(dc + 1) * T], yv[:], wcol, bcol,
                                       ALU.mult, ALU.add)

            c.step(("vector", yb2))

        def w1load(e, c, l, fc):
            par = "w" + str(fc % 2)
            e.dma_start(out=w1t[fc % 3][:].rearrange("p (dc f) -> p dc f", f=128),
                        in_=c.d["W1T4"][l].rearrange("(dc p) f -> p dc f", p=128)
                        [:, :, fc * 128:(fc + 1) * 128]).then_inc(c.S(par), 16)
            c.bump(par, 16)
            return _mark(e)

        c.step(("gpsimd", lambda e, c, l=l: w1load(e, c, l, 0)))
        c.step(("gpsimd", lambda e, c, l=l: w1load(e, c, l, 1)))
        for fc in range(NFC + 1):
            parts = []
            if fc < NFC:
                def fmm(e, c, l=l, fc=fc):
                    wdma(e, c, "w" + str(fc % 2))
                    r = None
                    ps = ps_bc1 if fc % 2 == 0 else ps_bc2
                    for th in range(2):
                        for dc in range(NDC):
                            r = e.matmul(ps[:, th * 512:(th + 1) * 512],
                                         w1t[fc % 3][:, dc * 128:(dc + 1) * 128],
                                         hb[:, dc * T + th * 512: dc * T + (th + 1) * 512],
                                         start=(dc == 0), stop=(dc == NDC - 1))
                    return r

                parts.append(("tensor", fmm))
                if fc + 2 < NFC:
                    parts.append(("gpsimd", lambda e, c, l=l, fc=fc:
                                  w1load(e, c, l, fc + 2)))
            if fc >= 1:
                def fev(e, c, l=l, fc=fc):
                    ps = ps_bc1 if (fc - 1) % 2 == 0 else ps_bc2
                    return e.activation(yh[:, (fc - 1) * T: fc * T], ps[:], AF.Silu,
                                        bias=ppar["b1"][:, l * 16 + fc - 1: l * 16 + fc])

                parts.append(("scalar", fev))
            c.step(*parts)

        def w2load(e, c, l, dc):
            par = "w" + str(dc % 2)
            e.dma_start(out=w2t[dc % 2][:].rearrange("p (fc d) -> p fc d", d=128),
                        in_=c.d["W2T4"][l].rearrange("(fc p) d -> p fc d", p=128)
                        [:, :, dc * 128:(dc + 1) * 128]).then_inc(c.S(par), 16)
            c.bump(par, 16)
            return _mark(e)

        c.step(("gpsimd", lambda e, c, l=l: w2load(e, c, l, 0)))
        for dc in range(NDC + 1):
            parts = []
            if dc < NDC:
                def f2mm(e, c, l=l, dc=dc):
                    wdma(e, c, "w" + str(dc % 2))
                    r = None
                    ps = ps_bc1 if dc % 2 == 0 else ps_bc2
                    for th in range(2):
                        for fc in range(NFC):
                            r = e.matmul(ps[:, th * 512:(th + 1) * 512],
                                         w2t[dc % 2][:, fc * 128:(fc + 1) * 128],
                                         yh[:, fc * T + th * 512: fc * T + (th + 1) * 512],
                                         start=(fc == 0), stop=(fc == NFC - 1))
                    return r

                parts.append(("tensor", f2mm))
                if dc + 1 < NDC:
                    parts.append(("gpsimd", lambda e, c, l=l, dc=dc:
                                  w2load(e, c, l, dc + 1)))
            if dc >= 1:
                def f2ev(e, c, l=l, dc=dc):
                    ps = ps_bc1 if (dc - 1) % 2 == 0 else ps_bc2
                    b2c = ppar["b2"][:, l * 4 + dc - 1: l * 4 + dc]
                    return e.scalar_tensor_tensor(h[:, (dc - 1) * T: dc * T], ps[:],
                                                  b2c, h[:, (dc - 1) * T: dc * T],
                                                  ALU.add, ALU.add)

                parts.append(("vector", f2ev))
            c.step(*parts)

        hdump(2 + 2 * l)
    feat_stats(c)
    for dc in range(NDC):
        def ybl(e, c, dc=dc):
            wcol = ppar["lnl_w"][:, dc:dc + 1]
            bcol = ppar["lnl_b"][:, dc:dc + 1]
            e.tensor_tensor(yv[:], h[:, dc * T:(dc + 1) * T], ps_bc1[:], ALU.add)
            e.tensor_tensor(yv[:], yv[:], ps_bc2[:], ALU.mult)
            return e.tensor_scalar(hb[:, dc * T:(dc + 1) * T], yv[:], wcol, bcol,
                                   ALU.mult, ALU.add)

        c.step(("vector", ybl))

    c.free("g5_Tc", "g5_Ts", "g5_Tfc", "g5_Tfs", "g5_ytr", "g5_yti",
           "g5_dr", "g5_di", "g5_yv", "g5_hsq", "g5_yh",
           "g5_w1t0", "g5_w1t1", "g5_w1t2", "g5_w2t0", "g5_w2t1")

    s1 = c.t["s1"] = c.sb("s1", [128, 8 * EMB], BF16, side="right")
    wot = [c.sb(f"g5_wot{i}", [128, 2048], BF16) for i in range(2)]

    def woload(e, c, os_):
        par = "w" + str(os_ % 2)
        e.dma_start(out=wot[os_ % 2][:].rearrange("p (dc o) -> p dc o", o=512),
                    in_=c.d["WoutT"].rearrange("(dc p) o -> p dc o", p=128)
                    [:, :, os_ * 512:(os_ + 1) * 512]).then_inc(c.S(par), 16)
        c.bump(par, 16)
        return _mark(e)

    c.step(("gpsimd", lambda e, c: woload(e, c, 0)))
    c.step(("gpsimd", lambda e, c: woload(e, c, 1)))
    for os_ in range(8):
        for tg in range(2):
            def womm(e, c, os_=os_, tg=tg):
                if tg == 0:
                    wdma(e, c, "w" + str(os_ % 2))
                r = None
                for i in range(4):
                    tc = tg * 4 + i
                    ps = (ps_a, ps_b, ps_bc1, ps_bc2)[i]
                    for dc in range(NDC):
                        e.matmul(ps[:, 0:512],
                                 hb[:, dc * T + tc * 128: dc * T + (tc + 1) * 128],
                                 wot[os_ % 2][:, dc * 512:(dc + 1) * 512],
                                 start=(dc == 0), stop=False)
                    r = e.matmul(ps[:, 0:512], onesr[:],
                                 boutr[:, os_ * 512:(os_ + 1) * 512],
                                 start=False, stop=True)
                return r

            c.step(("tensor", womm))

            def woev(e, c, os_=os_, tg=tg):
                r = None
                for i in range(4):
                    tc = tg * 4 + i
                    ps = (ps_a, ps_b, ps_bc1, ps_bc2)[i]
                    r = e.tensor_copy(
                        s1[:, tc * EMB + os_ * 512: tc * EMB + (os_ + 1) * 512],
                        ps[:, 0:512])
                return r

            wparts = [("vector", woev)]
            if tg == 1 and os_ + 2 < 8:
                wparts.append(("gpsimd", lambda e, c, os_=os_: woload(e, c, os_ + 2)))
            c.step(*wparts)

    if not FUSED:
        def s1store(e, c):
            for tc in range(8):
                e.dma_start(out=c.d["s1_d"][tc * 128:(tc + 1) * 128, :],
                            in_=s1[:, tc * EMB:(tc + 1) * EMB]).then_inc(c.S("so"), 16)
                c.bump("so", 16)
            wdma(e, c, "so")
            return _mark(e)

        c.step(("gpsimd", s1store))

    c.free("g5_wot0", "g5_wot1", "g5_h", "g5_hb", "g5_row", "g5_rowb",
           "g5_urho", "g5_dini", "g5_tails", "g5_recv", "g5_zero",
           "g5_pstage", *(("g5_pp_" + nm) for nm in pnames),
           "g5_id", "g5_idF", "g5_onesc", "g5_onesr", "g5_is2", "g5_isA",
           "g5_boutr", "g5_psa", "g5_psb", "g5_psbc1", "g5_psbc2", "g5_pspt")


def build_g6(c):
    nc = c.nc
    if not FUSED:
        c.t["s1"] = c.sb("s1", [128, 8 * EMB], BF16, side="right")
    c.t["xnT"] = c.sb("xnT_g6", [128, NEC * T], BF16, side="right")
    load_xnT(c, "xn1_bf")
    if not FUSED:
        def s1load(e, c):
            for tc in range(8):
                e.dma_start(out=c.t["s1"][:, tc * EMB:(tc + 1) * EMB],
                            in_=c.d["s1_d"][tc * 128:(tc + 1) * 128, :]
                            ).then_inc(c.S("pl"), 16)
                c.bump("pl", 16)
            wdma(e, c, "pl")
            return _mark(e)

        c.step(("gpsimd", s1load))
    xnT = c.t["xnT"]
    s1 = c.t["s1"]

    onesr = c.sb("g6_or", [1, 128], BF16)
    ballp = c.sb("g6_ballp", [128, EMB], BF16)
    bbr = c.sb("g6_bbr", [1, EMB], BF16)
    hw = c.sb("g6_hw", [128, T], BF16)
    ident = c.sb("g6_id", [128, 128], BF16)
    identF = c.sb("g6_idF", [128, 128], F32)
    sel = c.sb("g6_sel", [8, 128], BF16)
    id_d = nc.inline_tensor(np.eye(128, dtype=BF), name="g6ident").ap()
    idF_d = nc.inline_tensor(np.eye(128, dtype=np.float32), name="g6idF").ap()
    or_d = nc.inline_tensor(np.ones((1, 128), dtype=BF), name="g6ones").ap()
    sel_d = nc.inline_tensor(np.kron(np.eye(8), np.ones((1, 16))).astype(BF),
                             name="g6sel").ap()
    wgt = c.sb("g6_wgt", [128, NEC * 8], BF16)
    aall = c.sb("g6_aall", [128, NEC * 128], BF16)
    lgF = c.sb("g6_lgF", [8, T], F32)
    lgt = c.sb("g6_lgt", [128, 8], F32)
    lg2 = c.sb("g6_lg2", [128, 8], F32)
    m1 = c.sb("g6_m1", [128, 4], F32)
    route = c.sb("g6_route", [128, 8], F32)
    routeb = c.sb("g6_routeb", [128, 8], BF16)
    routeT = c.sb("g6_routeT", [8, T], BF16)
    hall = c.sb("g6_hall", [128, T], BF16)
    ps_a = c.ps("g6_psa", [128, 1024])
    ps_b = c.ps("g6_psb", [128, 1024])
    ps_t = c.ps("g6_pst", [128, 128])
    ps_tb = c.ps("g6_pstb", [8, 128], BF16)

    def prolog(e, c):
        for src, dst in ((id_d, ident), (idF_d, identF), (or_d, onesr),
                         (sel_d, sel), (c.d["bb_row"], bbr)):
            e.dma_start(out=dst[:], in_=src).then_inc(c.S("pl"), 16)
            c.bump("pl", 16)
        e.dma_start(out=wgt[:].rearrange("p (ec x) -> p ec x", x=8),
                    in_=c.d["WgT"].rearrange("(ec p) x -> p ec x", p=128)
                    ).then_inc(c.S("pl"), 16)
        e.dma_start(out=aall[:].rearrange("p (ec x) -> p ec x", x=128),
                    in_=c.d["AallT"].rearrange("(ec p) x -> p ec x", p=128)
                    ).then_inc(c.S("pl"), 16)
        e.dma_start(out=ballp[:], in_=c.d["BallP"]).then_inc(c.S("pl"), 16)
        c.bump("pl", 48)
        return _mark(e)

    c.step(("gpsimd", prolog))

    def lgmm(e, c):
        wdma(e, c, "pl")
        r = None
        for th in range(2):
            for ec in range(NEC):
                r = e.matmul(ps_a[0:8, th * 512:(th + 1) * 512],
                             wgt[:, ec * 8:(ec + 1) * 8],
                             xnT[:, ec * T + th * 512: ec * T + (th + 1) * 512],
                             start=(ec == 0), stop=(ec == NEC - 1))
        return r

    c.step(("tensor", lgmm))
    c.step(("vector", lambda e, c: VG(e, c, e.tensor_copy(lgF[:], ps_a[0:8, :])) and None))

    def hallmm(e, c):
        r = None
        for th in range(2):
            for ec in range(NEC):
                r = e.matmul(ps_b[:, th * 512:(th + 1) * 512],
                             aall[:, ec * 128:(ec + 1) * 128],
                             xnT[:, ec * T + th * 512: ec * T + (th + 1) * 512],
                             start=(ec == 0), stop=(ec == NEC - 1))
        return r

    c.step(("tensor", hallmm))
    c.step(("vector", lambda e, c: e.tensor_copy(hall[:], ps_b[:])))

    for tc in range(8):
        c.step(("tensor", lambda e, c, tc=tc:
                e.transpose(ps_t[:, 0:8], lgF[:, tc * 128:(tc + 1) * 128],
                            identF[0:8, 0:8])))

        def top2(e, c, tc=tc):
            VG(e, c, e.tensor_copy(lgt[:], ps_t[:, 0:8]))
            VG(e, c, e.tensor_reduce(m1[:, 0:1], lgt[:], AX.X, ALU.max), extra=True)
            VG(e, c, e.tensor_reduce(m1[:, 1:2], lgt[:], AX.X, ALU.max, negate=True),
               extra=True)
            VG(e, c, e.tensor_tensor(lg2[:], lgt[:],
                                     m1[:, 0:1].broadcast_to((128, 8)), ALU.is_ge))
            VG(e, c, e.tensor_scalar(lg2[:], lg2[:], 1e30, None, ALU.mult))
            VG(e, c, e.tensor_tensor(lg2[:], lgt[:], lg2[:], ALU.subtract))
            VG(e, c, e.tensor_reduce(m1[:, 2:3], lg2[:], AX.X, ALU.max), extra=True)
            return None

        c.step(("vector", top2))
        c.step(("scalar", lambda e, c: VG(e, c, e.activation(lg2[:], lgt[:], AF.Exp,
                                                              bias=m1[:, 1:2])) and None))

        def routev(e, c, tc=tc):
            VG(e, c, e.tensor_tensor(route[:], lgt[:],
                                     m1[:, 2:3].broadcast_to((128, 8)), ALU.is_ge))
            VG(e, c, e.tensor_tensor(route[:], route[:], lg2[:], ALU.mult))
            VG(e, c, e.tensor_reduce(m1[:, 3:4], route[:], AX.X, ALU.add), extra=True)
            VG(e, c, e.reciprocal(m1[:, 3:4], m1[:, 3:4]), extra=True)
            VG(e, c, e.tensor_scalar(route[:], route[:], m1[:, 3:4], None, ALU.mult))
            VG(e, c, e.tensor_copy(routeb[:], route[:]))
            return None

        c.step(("vector", routev))
        c.step(("tensor", lambda e, c: e.transpose(ps_tb[:, 0:128], routeb[:],
                                                   ident[:])))
        c.step(("vector", lambda e, c, tc=tc:
                VG(e, c, e.tensor_copy(routeT[:, tc * 128:(tc + 1) * 128],
                                       ps_tb[0:8, :])) and None))

    def selmm(e, c):
        e.matmul(ps_a[:, 0:512], sel[:], routeT[:, 0:512], start=True, stop=True)
        return e.matmul(ps_a[:, 512:1024], sel[:], routeT[:, 512:1024],
                        start=True, stop=True)

    c.step(("tensor", selmm))
    c.step(("vector", lambda e, c: e.tensor_tensor(hw[:], hall[:], ps_a[:], ALU.mult)))
    c.free("g6_psa", "g6_psb", "g6_pst", "g6_pstb", "g6_lgF", "g6_lgt",
           "g6_lg2", "g6_m1", "g6_route", "g6_routeb", "g6_routeT", "g6_hall",
           "g6_wgt", "g6_aall", "g6_id", "g6_idF", "g6_sel")

    wbt = [c.sb(f"g6_wbt{i}", [128, 512], BF16) for i in range(3)]
    silu_t = c.sb("g6_silu", [128, 512], BF16)
    sg = c.sb("g6_sg", [128, 512], F32)
    xts = [c.sb(f"g6_xt{i}", [128, 512], F32) for i in range(2)]
    out_t = [c.sb(f"g6_out{i}", [128, 512], F32) for i in range(2)]
    ps8 = [c.ps(f"g6_ps{i}", [128, 512]) for i in range(8)]

    def wbload(e, c, i):
        os_, ec = i // NEC, i % NEC
        slot = i % 3
        par = "w" + str(i % 2)
        e.dma_start(out=wbt[slot][:],
                    in_=c.d["WbT"][os_, ec * 128:(ec + 1) * 128, :]
                    ).then_inc(c.S(par), 16)
        c.bump(par, 16)
        return _mark(e)

    c.step(("gpsimd", lambda e, c: wbload(e, c, 0)))
    c.step(("gpsimd", lambda e, c: wbload(e, c, 1)))
    for os_ in range(8):
        for ec in range(NEC):
            i = os_ * NEC + ec

            def mm(e, c, i=i, ec=ec):
                slot = i % 3
                wdma(e, c, "w" + str(i % 2))
                r = None
                for tc in range(8):
                    r = e.matmul(ps8[tc][:],
                                 xnT[:, ec * T + tc * 128: ec * T + (tc + 1) * 128],
                                 wbt[slot][:], start=(ec == 0), stop=False)
                return r

            parts = [("tensor", mm)]
            if i + 2 < 8 * NEC:
                parts.append(("gpsimd", lambda e, c, i=i: wbload(e, c, i + 2)))
            c.step(*parts)

        def dmm(e, c, os_=os_):
            r = None
            for tc in range(8):
                e.matmul(ps8[tc][:], hw[:, tc * 128:(tc + 1) * 128],
                         ballp[:, os_ * 512:(os_ + 1) * 512], start=False, stop=False)
                r = e.matmul(ps8[tc][:], onesr[:], bbr[:, os_ * 512:(os_ + 1) * 512],
                             start=False, stop=True)
            return r

        c.step(("tensor", dmm))

        def xload(e, c, os_, tc):
            par = "x" + str(tc % 2)
            e.dma_start(out=xts[tc % 2][:],
                        in_=c.d["xnew_d"][tc * 128:(tc + 1) * 128,
                                          os_ * 512:(os_ + 1) * 512]
                        ).then_inc(c.S(par), 16)
            c.bump(par, 16)
            return _mark(e)

        c.step(("sync", lambda e, c, os_=os_: xload(e, c, os_, 0)))
        c.step(("sync", lambda e, c, os_=os_: xload(e, c, os_, 1)))
        for tc in range(8):
            sparts = [("scalar", lambda e, c, tc=tc:
                       e.activation(silu_t[:], ps8[tc][:], AF.Silu))]
            if 2 <= tc + 1 < 8:
                sparts.append(("sync", lambda e, c, os_=os_, tc=tc:
                               xload(e, c, os_, tc + 1)))
            c.step(*sparts)

            def ep(e, c, os_=os_, tc=tc):
                slot = tc % 2
                wdma(e, c, "x" + str(slot))
                e.tensor_tensor(sg[:], silu_t[:],
                                s1[:, tc * EMB + os_ * 512: tc * EMB + (os_ + 1) * 512],
                                ALU.mult)
                wdma(e, c, "s" + str(slot))
                return e.tensor_tensor(out_t[slot][:], xts[slot][:], sg[:], ALU.add)

            c.step(("vector", ep))

            def ostore(e, c, os_=os_, tc=tc):
                slot = tc % 2
                par = "s" + str(slot)
                e.dma_start(out=c.d["out"][tc * 128:(tc + 1) * 128,
                                           os_ * 512:(os_ + 1) * 512],
                            in_=out_t[slot][:]).then_inc(c.S(par), 16)
                c.bump(par, 16)
                return _mark(e)

            c.step(("gpsimd", ostore))

    def fin(e, c):
        wdma(e, c, "s0")
        wdma(e, c, "s1")
        return _mark(e)

    c.step(("gpsimd", fin))


BUILDERS = {"g1": build_g1, "g2": build_g2, "g3": build_g3, "g4": build_g4,
            "g5": build_g5, "g6": build_g6}
PERSIST = {
    "g1": [("xnT", [128, NEC * T], BF16)],
    "g2": [("s0T", [128, 8 * 1024], BF16)],
    "g3": [("xnT", [128, NEC * T], BF16)],
    "g4": [("s0T", [128, 8 * 1024], BF16)],
    "g5": [("xnT", [128, NEC * T], BF16)],
    "g6": [],
}


def build_program(groups, host, fused):
    nc = bass.Bass("TRN2", target_bir_lowering=False)
    c = Ctx(nc)
    decl_io(c, groups, host, fused)
    for g in groups:
        for nm, shape, dt in PERSIST[g]:
            if nm not in c.t:
                c.t[nm] = c.sb(nm, shape, dt, side="right")
    with nc.allow_low_precision(reason="bf16 matmul operands by design"), \
            nc.Block() as block:
        c.block = block
        for g in groups:
            BUILDERS[g](c)
    return c


def _in_map(groups, host, core):
    m = {}
    for g in groups:
        for k in G_INPUTS[g]:
            m.setdefault(k, host[k])
        if "x" in G_READS[g]:
            m["x"] = host["x"][core]
        if g == "g5":
            m["is2"] = host["is2"][core]
    return m


def kernel(**inputs):
    host = host_prep(inputs)
    if FUSED:
        groups = [g for gs in GROUP_SETS for g in gs]
        c = build_program(groups, host, fused=True)
        in_maps = [_in_map(groups, host, core) for core in range(8)]
        res = run_bass_kernel_spmd(c.nc, in_maps, core_ids=list(range(8)))
        outs = [np.asarray(res.results[core]["out"]) for core in range(8)]
    else:
        handoff = [dict() for _ in range(8)]
        outs = [None] * 8
        for gs in GROUP_SETS:
            c = build_program(gs, host, fused=False)
            reads, writes = set(), set()
            for g in gs:
                reads |= set(G_READS[g]) - {"x"}
                writes |= set(G_WRITES[g])
            reads -= writes
            in_maps = []
            for core in range(8):
                m = _in_map(gs, host, core)
                for nm in reads:
                    m[nm] = handoff[core][nm]
                in_maps.append(m)
            res = run_bass_kernel_spmd(c.nc, in_maps, core_ids=list(range(8)))
            for core in range(8):
                for nm in writes:
                    handoff[core][nm] = np.asarray(res.results[core][nm])
                if "g6" in gs:
                    outs[core] = np.asarray(res.results[core]["out"])
    full = np.empty((4, 2048, EMB), np.float32)
    for core in range(8):
        b, hh = core // 2, core % 2
        full[b, hh * T:(hh + 1) * T, :] = outs[core]
    return full


# version 05
# speedup vs baseline: 1.1193x; 1.1193x over previous
import numpy as np
import ml_dtypes
import concourse.bass as bass
import concourse.mybir as mybir
from concourse.bass_utils import run_bass_kernel_spmd

F32 = mybir.dt.float32
BF16 = mybir.dt.bfloat16
AF = mybir.ActivationFunctionType
ALU = mybir.AluOpType
AX = mybir.AxisListType
BF = ml_dtypes.bfloat16

T = 1024
EMB = 4096
NEC = 32
HID = 11008
NHC = 86
D1 = 512
NDC = 4
DFF = 2048
NFC = 16
DEPTH1 = 4
SCALE_QK = float(1.0 / np.float32(np.sqrt(np.float32(128.0))))
EPS = 1e-5

FUSED = True
DEBUG_G5 = False
GROUP_SETS = [["g1"], ["g2"], ["g3"], ["g4"], ["g5"], ["g6"]]


class Ctx:
    def __init__(self, nc):
        self.nc = nc
        self.block = None
        self.sem = {}
        self.cnt = {}
        self.t = {}
        self.d = {}
        self.guards = {}

    def S(self, name):
        if name not in self.sem:
            self.sem[name] = self.nc.alloc_semaphore(name)
            self.cnt[name] = 0
        return self.sem[name]

    def bump(self, name, amt):
        self.S(name)
        self.cnt[name] += amt
        return self.cnt[name]

    def step(self, *parts):
        merged, order = {}, []
        for en, f in parts:
            if f is None:
                continue
            if en in merged:
                old = merged[en]
                merged[en] = (lambda e, c, a=old, b=f: (a(e, c), b(e, c))[1])
            else:
                merged[en] = f
                order.append(en)
        if not order:
            return
        base = self.cnt.get("step", 0)
        self.S("step")
        for en in order:
            def body(e, fn=merged[en], base=base):
                if base:
                    e.wait_ge(self.S("step"), base)
                last = fn(e, self)
                if last is None:
                    last = _mark(e)
                last.then_inc(self.S("step"), 1)

            getattr(self.block, en)(body)
        self.cnt["step"] = base + len(order)

    def sb(self, name, shape, dtype, side=None):
        g = self.nc.sbuf_tensor(name, shape, dtype, side=side)
        h = g.__enter__()
        self.guards[name] = (len(self.guards), g)
        return h

    def ps(self, name, shape, dtype=F32):
        g = self.nc.psum_tensor(name, shape, dtype)
        h = g.__enter__()
        self.guards[name] = (len(self.guards), g)
        return h

    def free(self, *names):
        for nm in sorted(names, key=lambda n: -self.guards[n][0]):
            self.guards.pop(nm)[1].__exit__(None, None, None)



def _mark(e):
    try:
        return e.engine_nop()
    except AttributeError:
        return e.nop()


def VG(e, c, instr, extra=False):
    c.S("vs")
    c.cnt["vs"] += 1
    instr.then_inc(c.S("vs"), 1)
    e.wait_ge(c.S("vs"), c.cnt["vs"])
    if extra:
        n = _mark(e)
        c.cnt["vs"] += 1
        n.then_inc(c.S("vs"), 1)
        e.wait_ge(c.S("vs"), c.cnt["vs"])
    return instr


def wdma(e, c, name):
    e.wait_ge(c.S(name), c.cnt.get(name, 0))


def host_prep(inp):
    H = {}
    c = np.ascontiguousarray
    x = np.asarray(inp["L0_attn_x"], dtype=np.float32)
    H["x"] = [c(x[b, h * T:(h + 1) * T, :]) for b in range(4) for h in range(2)]

    wq = np.asarray(inp["L0_attn_Wq"], np.float32)
    H["WqJ"] = c(wq.reshape(EMB, 128, 8, 4).transpose(3, 0, 2, 1).reshape(4, EMB, 1024))
    H["bq"] = c(np.asarray(inp["L0_attn_bq"], np.float32))
    H["WkG"] = c(np.asarray(inp["L0_attn_Wk"], np.float32).transpose(2, 0, 1))
    H["bk"] = c(np.asarray(inp["L0_attn_bk"], np.float32))
    H["WvG"] = c(np.asarray(inp["L0_attn_Wv"], np.float32).transpose(2, 0, 1))
    H["bv"] = c(np.asarray(inp["L0_attn_bv"], np.float32))

    H["w1c"] = c(np.asarray(inp["L0_ffn_w1"], np.float32).T
                 .reshape(EMB, NHC, 128).transpose(1, 0, 2))
    H["w3c"] = c(np.asarray(inp["L0_ffn_w3"], np.float32).T
                 .reshape(EMB, NHC, 128).transpose(1, 0, 2))
    H["w2s"] = c(np.asarray(inp["L0_ffn_w2"], np.float32).T
                 .reshape(HID, 8, 512).transpose(1, 0, 2))
    H["A1T"] = c(np.asarray(inp["L0_ffn_lora_A1"], np.float32).T)
    H["B1T"] = c(np.asarray(inp["L0_ffn_lora_B1"], np.float32).T)
    H["A3T"] = c(np.asarray(inp["L0_ffn_lora_A3"], np.float32).T)
    H["B3T"] = c(np.asarray(inp["L0_ffn_lora_B3"], np.float32).T)
    H["A2T"] = c(np.asarray(inp["L0_ffn_lora_A2"], np.float32).T)
    H["B2T"] = c(np.asarray(inp["L0_ffn_lora_B2"], np.float32).T)

    H["WinT"] = c(np.asarray(inp["L1_attn_Win"], np.float32).T)
    for nm in ("ln1_w", "ln1_b", "phz_re", "phz_im", "lc_re", "lc_im",
               "sc_lin", "ln2_w", "ln2_b", "b2"):
        H["p_" + nm] = c(np.asarray(inp["L1_attn_" + nm], np.float32).reshape(16, 128))
    H["p_b1"] = c(np.asarray(inp["L1_attn_b1"], np.float32).reshape(64, 128))
    H["p_bin"] = c(np.asarray(inp["L1_attn_bin"], np.float32).reshape(4, 128))
    H["p_lnl_w"] = c(np.asarray(inp["L1_attn_lnl_w"], np.float32).reshape(4, 128))
    H["p_lnl_b"] = c(np.asarray(inp["L1_attn_lnl_b"], np.float32).reshape(4, 128))
    H["W1T4"] = c(np.asarray(inp["L1_attn_W1"], np.float32).transpose(0, 2, 1))
    H["W2T4"] = c(np.asarray(inp["L1_attn_W2"], np.float32).transpose(0, 2, 1))
    H["WoutT"] = c(np.asarray(inp["L1_attn_Wout"], np.float32).T)
    H["bout_row"] = c(np.asarray(inp["L1_attn_bout"], np.float32).reshape(1, EMB))

    H["WbT"] = c(np.asarray(inp["L1_ffn_W_base"], np.float32).T
                 .reshape(EMB, 8, 512).transpose(1, 0, 2))
    H["bb_row"] = c(np.asarray(inp["L1_ffn_b_base"], np.float32).reshape(1, EMB))
    H["WgT"] = c(np.asarray(inp["L1_ffn_W_gate"], np.float32).T)
    H["AallT"] = c(np.asarray(inp["L1_ffn_A"], np.float32).reshape(128, EMB).T)
    H["BallP"] = c(np.asarray(inp["L1_ffn_B"], np.float32).transpose(0, 2, 1).reshape(128, EMB))

    H["is2"] = [np.full((128, 1), float(core % 2), np.float32) for core in range(8)]
    return H


G_INPUTS = {
    "g1": ["WqJ", "bq", "WkG", "bk", "WvG", "bv"],
    "g2": [],
    "g3": ["w1c", "w3c", "A1T", "B1T", "A3T", "B3T"],
    "g4": ["w2s", "B2T", "A2T"],
    "g5": ["WinT", "p_ln1_w", "p_ln1_b", "p_phz_re", "p_phz_im", "p_lc_re",
           "p_lc_im", "p_sc_lin", "p_ln2_w", "p_ln2_b", "p_b1", "p_b2",
           "p_bin", "p_lnl_w", "p_lnl_b", "W1T4", "W2T4", "WoutT", "bout_row"],
    "g6": ["WbT", "bb_row", "WgT", "AallT", "BallP"],
}
HANDOFFS = {
    "xn_bf": ([T, EMB], BF16),
    "qg": ([8, 128, T], BF16),
    "kg": ([8, 128, T], BF16),
    "vg": ([8, 128, T], BF16),
    "negc8": ([1024], BF16),
    "s0": ([T, 1024], BF16),
    "gdram": ([HID, T], BF16),
    "h2lo_d": ([16, T], F32),
    "xnew_d": ([T, EMB], F32),
    "xn1_bf": ([T, EMB], BF16),
    "s1_d": ([T, EMB], BF16),
}
G_READS = {
    "g1": ["x"], "g2": ["qg", "kg", "vg", "negc8"],
    "g3": ["xn_bf"], "g4": ["gdram", "s0", "x"],
    "g5": ["xnew_d"], "g6": ["xn1_bf", "s1_d", "xnew_d"],
}
G_WRITES = {
    "g1": ["xn_bf", "qg", "kg", "vg", "negc8"],
    "g2": ["s0"], "g3": ["gdram"], "g4": ["xnew_d", "h2lo_d"],
    "g5": ["xn1_bf", "s1_d"], "g6": [],
}


def decl_io(c, groups, host, fused):
    nc = c.nc
    reads, writes, weights = set(), set(), []
    for g in groups:
        weights += [k for k in G_INPUTS[g] if k not in weights]
        reads |= set(G_READS[g])
        writes |= set(G_WRITES[g])
    for k in weights:
        c.d[k] = nc.dram_tensor(k, list(host[k].shape), F32, kind="ExternalInput").ap()
    if "x" in reads:
        c.d["x"] = nc.dram_tensor("x", [T, EMB], F32, kind="ExternalInput").ap()
    if "g5" in groups:
        c.d["is2"] = nc.dram_tensor("is2", [128, 1], F32, kind="ExternalInput").ap()
    if "g6" in groups:
        c.d["out"] = nc.dram_tensor("out", [T, EMB], F32, kind="ExternalOutput").ap()
    if DEBUG_G5 and "g5" in groups:
        c.d["h_dbg"] = nc.dram_tensor("h_dbg", [128, 10 * NDC * T], F32,
                                      kind="ExternalOutput").ap()
        c.d["s_dbg"] = nc.dram_tensor("s_dbg", [128, 8 * T], F32,
                                      kind="ExternalOutput").ap()
    for nm in (reads | writes) - {"x"}:
        shape, dt = HANDOFFS[nm]
        if fused:
            c.d[nm] = nc.dram_tensor(nm, shape, dt).ap()
        else:
            kind = "ExternalOutput" if nm in writes else "ExternalInput"
            c.d[nm] = nc.dram_tensor(nm, shape, dt, kind=kind).ap()


def ln_global(c, src_name, dst_name, tag):
    xt = [c.sb(tag + "_xt" + str(i), [128, EMB], F32) for i in range(2)]
    sq = c.sb(tag + "_sq", [128, EMB], F32)
    st = c.sb(tag + "_st", [128, 8], F32)
    xnt = [c.sb(tag + "_xnt" + str(i), [128, EMB], BF16) for i in range(2)]

    def load(e, c, tc):
        par = "x" + str(tc % 2)
        e.dma_start(out=xt[tc % 2][:],
                    in_=c.d[src_name][tc * 128:(tc + 1) * 128, :]
                    ).then_inc(c.S(par), 16)
        c.bump(par, 16)
        return _mark(e)

    c.step(("sync", lambda e, c: load(e, c, 0)))
    for tc in range(8):
        par = "x" + str(tc % 2)

        def sums(e, c, tc=tc, par=par):
            wdma(e, c, par)
            return e.activation(sq[:], xt[tc % 2][:], AF.Square)

        def rsum(e, c, tc=tc, par=par):
            wdma(e, c, par)
            VG(e, c, e.tensor_reduce(st[:, 0:1], xt[tc % 2][:], AX.X, ALU.add),
               extra=True)
            return None

        nxt = (lambda e, c, tc=tc: load(e, c, tc + 1)) if tc < 7 else None
        c.step(("scalar", sums), ("vector", rsum), ("sync", nxt))

        def stats(e, c, tc=tc):
            VG(e, c, e.tensor_reduce(st[:, 4:5], sq[:], AX.X, ALU.add), extra=True)
            VG(e, c, e.tensor_scalar(st[:, 2:3], st[:, 0:1], -1.0 / EMB, None, ALU.mult))
            VG(e, c, e.tensor_scalar(st[:, 3:4], st[:, 4:5], 1.0 / EMB, None, ALU.mult))
            VG(e, c, e.tensor_tensor(st[:, 1:2], st[:, 2:3], st[:, 2:3], ALU.mult))
            VG(e, c, e.tensor_tensor(st[:, 5:6], st[:, 3:4], st[:, 1:2], ALU.subtract))
            VG(e, c, e.tensor_scalar(st[:, 5:6], st[:, 5:6], EPS, None, ALU.add))
            return None

        c.step(("vector", stats))
        c.step(("scalar", lambda e, c: VG(e, c, e.activation(st[:, 6:7],
                                                              st[:, 5:6], AF.Sqrt)) and None))

        def norm(e, c, tc=tc):
            wdma(e, c, "s" + str(tc % 2))
            VG(e, c, e.reciprocal(st[:, 7:8], st[:, 6:7]), extra=True)
            return e.tensor_scalar(xnt[tc % 2][:], xt[tc % 2][:], st[:, 2:3],
                                   st[:, 7:8], ALU.add, ALU.mult)

        c.step(("vector", norm))

        def store(e, c, tc=tc):
            par = "s" + str(tc % 2)
            e.dma_start(out=c.d[dst_name][tc * 128:(tc + 1) * 128, :],
                        in_=xnt[tc % 2][:]).then_inc(c.S(par), 16)
            c.bump(par, 16)
            return _mark(e)

        c.step(("gpsimd", store))

    def fin(e, c):
        wdma(e, c, "s0")
        wdma(e, c, "s1")
        return _mark(e)

    c.step(("gpsimd", fin))
    c.free(*(tag + "_xt" + str(i) for i in range(2)), tag + "_sq", tag + "_st",
           *(tag + "_xnt" + str(i) for i in range(2)))


def load_xnT(c, src_name):
    xnT = c.t["xnT"]

    def tp(e, c):
        for ec in range(NEC):
            e.dma_start(out=xnT[:, ec * T:(ec + 1) * T],
                        in_=c.d[src_name][:, ec * 128:(ec + 1) * 128],
                        transpose=True).then_inc(c.S("tp"), 16)
            c.bump("tp", 16)
        wdma(e, c, "tp")
        return _mark(e)

    c.step(("sync", tp))


def build_g1(c):
    nc = c.nc
    ln_global(c, "x", "xn_bf", "l0")
    load_xnT(c, "xn_bf")

    xnT = c.t["xnT"]
    wq4 = [c.sb(f"g1_wq4{i}", [128, 4096], BF16) for i in range(2)]
    wqt = c.sb("g1_wqt", [128, NEC * 1024], BF16)
    wkb = [c.sb(f"g1_wkb{i}", [128, 4096], BF16) for i in range(2)]
    bqs = c.sb("g1_bqs", [128, 32], F32)
    bq8 = c.sb("g1_bq8", [128, 8], F32)
    bkv = c.sb("g1_bkv", [128, 16], F32)
    evac = [c.sb(f"g1_evac{i}", [128, 1024], BF16) for i in range(3)]
    sqs = [c.sb(f"g1_sqs{i}", [128, 1024], F32) for i in range(2)]
    qacc = c.sb("g1_qacc", [128, T], F32)
    kacc = c.sb("g1_kacc", [128, T], F32)
    accb = c.sb("g1_accb", [128, T], BF16)
    mxq = c.sb("g1_mxq", [128, 8], F32)
    mxk = c.sb("g1_mxk", [128, 8], F32)
    negc_c = c.sb("g1_negc", [128, 8], BF16)
    negcT = c.sb("g1_negcT", [8, 128], BF16)
    ident = c.sb("g1_id", [128, 128], BF16)
    ps_a = c.ps("g1_psa", [128, 1024])
    ps_b = c.ps("g1_psb", [128, 1024])
    ps_t = c.ps("g1_pst", [128, 512], BF16)
    id_d = nc.inline_tensor(np.eye(128, dtype=BF), name="ident128").ap()

    def prolog(e, c):
        e.dma_start(out=bqs[:], in_=c.d["bq"]).then_inc(c.S("pl"), 16)
        e.dma_start(out=bkv[:, 0:8], in_=c.d["bk"]).then_inc(c.S("pl"), 16)
        e.dma_start(out=bkv[:, 8:16], in_=c.d["bv"]).then_inc(c.S("pl"), 16)
        e.dma_start(out=ident[:], in_=id_d).then_inc(c.S("pl"), 16)
        c.bump("pl", 64)
        return _mark(e)

    def wqload(e, c, ec):
        par = "w" + str(ec % 2)
        for j in range(4):
            e.dma_start(out=wq4[ec % 2][:, j * 1024:(j + 1) * 1024],
                        in_=c.d["WqJ"][j, ec * 128:(ec + 1) * 128, :]
                        ).then_inc(c.S(par), 16)
        c.bump(par, 64)
        return _mark(e)

    c.step(("gpsimd", prolog))
    c.step(("gpsimd", lambda e, c: wqload(e, c, 0)))

    def bqr(e, c):
        wdma(e, c, "pl")
        VG(e, c, e.tensor_reduce(bq8[:].rearrange("p (g o) -> p g o", g=8),
                                 bqs[:].rearrange("p (g j) -> p g j", g=8),
                                 AX.X, ALU.add), extra=True)
        VG(e, c, e.tensor_scalar(bq8[:], bq8[:], SCALE_QK, None, ALU.mult))
        e.memset(qacc[:], 0.0)
        return e.memset(kacc[:], 0.0)

    c.step(("vector", bqr))

    for ec in range(NEC):
        def red(e, c, ec=ec):
            wdma(e, c, "w" + str(ec % 2))
            VG(e, c, e.tensor_reduce(
                wqt[:, ec * 1024:(ec + 1) * 1024],
                wq4[ec % 2][:].rearrange("p (j gd) -> p gd j", j=4),
                AX.X, ALU.add), extra=True)
            return None

        c.step(("vector", red),
               ("gpsimd", (lambda e, c, ec=ec: wqload(e, c, ec + 1))
                if ec + 1 < NEC else None))

    def kvload(e, c, idx):
        wname = "WkG" if idx < 8 else "WvG"
        par = "w" + str(idx % 2)
        e.dma_start(out=wkb[idx % 2][:].rearrange("p (ec col) -> p ec col", col=128),
                    in_=c.d[wname][idx % 8].rearrange("(ec p) col -> p ec col", p=128)
                    ).then_inc(c.S(par), 16)
        c.bump(par, 16)
        return _mark(e)

    c.step(("gpsimd", lambda e, c: kvload(e, c, 0)))

    NP = 24

    def mms(e, c, i):
        if i < 16:
            wdma(e, c, "w" + str(i % 2))
            w = wkb[i % 2]
            cols = [w[:, ec * 128:(ec + 1) * 128] for ec in range(NEC)]
        else:
            g = i - 16
            cols = [wqt[:, ec * 1024 + g * 128: ec * 1024 + (g + 1) * 128]
                    for ec in range(NEC)]
        ps = ps_a if i % 2 == 0 else ps_b
        r = None
        for th in range(2):
            for ec in range(NEC):
                r = e.matmul(ps[:, th * 512:(th + 1) * 512], cols[ec],
                             xnT[:, ec * T + th * 512: ec * T + (th + 1) * 512],
                             start=(ec == 0), stop=(ec == NEC - 1))
        return r

    def ev(e, c, i):
        ps = ps_a if i % 2 == 0 else ps_b
        wdma(e, c, "so")
        if i < 16:
            return e.tensor_scalar(evac[i % 3][:], ps[:], bkv[:, i:i + 1], None,
                                   ALU.add)
        return e.tensor_scalar(evac[i % 3][:], ps[:], SCALE_QK,
                               bq8[:, i - 16:i - 15], ALU.mult, ALU.add)

    def sq(e, c, i):
        if i < 8 or i >= 16:
            return e.activation(sqs[i % 2][:], evac[i % 3][:], AF.Square)
        return None

    def acc(e, c, i):
        dst = kacc if i < 8 else qacc
        return e.tensor_tensor(dst[:], dst[:], sqs[i % 2][:], ALU.add)

    def store(e, c, i):
        dsts = {0: "kg", 1: "vg", 2: "qg"}
        e.dma_start(out=c.d[dsts[i // 8]][i % 8], in_=evac[i % 3][:]
                    ).then_inc(c.S("so"), 16)
        c.bump("so", 16)
        return _mark(e)

    for i in range(NP + 3):
        parts = []
        if i < NP:
            parts.append(("tensor", lambda e, c, i=i: mms(e, c, i)))
            if i + 1 < 16:
                parts.append(("gpsimd", lambda e, c, i=i: kvload(e, c, i + 1)))
        if 0 <= i - 1 < NP:
            parts.append(("vector", lambda e, c, i=i: ev(e, c, i - 1)))
        if 0 <= i - 2 < NP and (i - 2 < 8 or i - 2 >= 16):
            parts.append(("scalar", lambda e, c, i=i: sq(e, c, i - 2)))
        if 0 <= i - 2 < NP:
            parts.append(("gpsimd", lambda e, c, i=i: store(e, c, i - 2)))
        if 0 <= i - 3 < NP and (i - 3 < 8 or i - 3 >= 16):
            parts.append(("vector", lambda e, c, i=i: acc(e, c, i - 3)))
        c.step(*parts)

    c.step(("vector", lambda e, c: e.tensor_copy(accb[:], qacc[:])))
    for src_i in range(2):
        for half in range(2):
            def tps(e, c, half=half):
                r = None
                for blk in range(4):
                    r = e.transpose(ps_t[:, blk * 128:(blk + 1) * 128],
                                    accb[:, (half * 4 + blk) * 128:
                                         (half * 4 + blk + 1) * 128], ident[:])
                return r

            c.step(("tensor", tps))

            def mxs(e, c, src_i=src_i, half=half):
                dst = mxq if src_i == 0 else mxk
                for blk in range(4):
                    VG(e, c, e.tensor_reduce(dst[:, half * 4 + blk:half * 4 + blk + 1],
                                             ps_t[:, blk * 128:(blk + 1) * 128],
                                             AX.X, ALU.max), extra=True)
                if src_i == 0 and half == 1:
                    return e.tensor_copy(accb[:], kacc[:])
                return None

            c.step(("vector", mxs))

    c.step(("vector", lambda e, c: VG(e, c, e.tensor_tensor(mxq[:], mxq[:],
                                                             mxk[:], ALU.mult)) and None))
    c.step(("scalar", lambda e, c: VG(e, c, e.activation(mxk[:], mxq[:], AF.Sqrt)) and None))
    c.step(("vector", lambda e, c: VG(e, c, e.tensor_scalar(negc_c[:], mxk[:], -1.0,
                                                            None, ALU.mult)) and None))
    c.step(("tensor", lambda e, c: e.transpose(ps_t[0:8, 0:128], negc_c[:], ident[:])))
    c.step(("vector", lambda e, c: VG(e, c, e.tensor_copy(negcT[:], ps_t[0:8, 0:128])) and None))

    def nstore(e, c):
        e.dma_start(out=c.d["negc8"].rearrange("(a b) -> a b", a=8),
                    in_=negcT[:]).then_inc(c.S("so"), 16)
        c.bump("so", 16)
        wdma(e, c, "so")
        return _mark(e)

    c.step(("gpsimd", nstore))
    c.free("g1_wq40", "g1_wq41", "g1_wqt", "g1_wkb0", "g1_wkb1", "g1_bqs",
           "g1_bq8", "g1_bkv", "g1_evac0", "g1_evac1", "g1_evac2", "g1_sqs0", "g1_sqs1",
           "g1_qacc", "g1_kacc", "g1_accb", "g1_mxq", "g1_mxk", "g1_negc",
           "g1_negcT", "g1_id", "g1_psa", "g1_psb", "g1_pst")


def build_g2(c):
    nc = c.nc
    GRP = 16
    NG = T // GRP
    NSB = NG * 4

    crow = c.sb("g2_crow", [1, T], BF16)
    cbc = c.sb("g2_cbc", [128, T], F32)
    onesr = c.sb("g2_ones", [1, 128], BF16)
    ident = c.sb("g2_id", [128, 128], BF16)
    id_d = nc.inline_tensor(np.eye(128, dtype=BF), name="g2ident").ap()
    ones_d = nc.inline_tensor(np.ones((1, 128), dtype=BF), name="g2ones").ap()
    ps_c = c.ps("g2_psc", [128, 1024])

    def prolog(e, c):
        e.dma_start(out=ident[:], in_=id_d).then_inc(c.S("pl"), 16)
        e.dma_start(out=onesr[:], in_=ones_d).then_inc(c.S("pl"), 16)
        e.dma_start(out=crow[:], in_=c.d["negc8"].rearrange("(a b) -> a b", a=1)
                    ).then_inc(c.S("pl"), 16)
        c.bump("pl", 48)
        return _mark(e)

    c.step(("gpsimd", prolog))

    def cb_mm(e, c):
        wdma(e, c, "pl")
        e.matmul(ps_c[:, 0:512], onesr[:], crow[:, 0:512], start=True, stop=True)
        return e.matmul(ps_c[:, 512:1024], onesr[:], crow[:, 512:1024],
                        start=True, stop=True)

    c.step(("tensor", cb_mm))
    c.step(("vector", lambda e, c: e.tensor_copy(cbc[:], ps_c[:])))
    c.free("g2_psc")

    qt = [c.sb(f"g2_qt{i}", [8, 128 * GRP], BF16) for i in range(3)]
    kt = [c.sb(f"g2_kt{i}", [8, 128 * GRP], BF16) for i in range(3)]
    vt = [c.sb(f"g2_vt{i}", [128, 9 * GRP], BF16) for i in range(3)]
    expt = [c.sb(f"g2_ex{i}", [128, 512], BF16) for i in range(2)]
    subs = [c.sb(f"g2_sub{i}", [128, 512], F32) for i in range(2)]
    r32 = c.sb("g2_r32", [128, 32], F32)
    ot = c.sb("g2_ot", [128, T * 8], BF16)
    s0T = c.t["s0T"]
    ps_s = [c.ps(f"g2_pss{i}", [128, 512]) for i in range(2)]
    ps_o = [c.ps(f"g2_pso{i}", [128, 512]) for i in range(2)]
    ps_t = c.ps("g2_pst", [128, 512], BF16)

    def vinit(e, c):
        e.memset(vt[0][:], 1.0)
        e.memset(vt[1][:], 1.0)
        return e.memset(vt[2][:], 1.0)

    c.step(("vector", vinit))

    def qkvload(e, c, grp):
        b = grp % 3
        par = "w" + str(grp % 2)
        t0 = grp * GRP
        e.dma_start(out=qt[b][:].rearrange("g (d t) -> g d t", t=GRP),
                    in_=c.d["qg"][:, :, t0:t0 + GRP]).then_inc(c.S(par), 16)
        e.dma_start(out=kt[b][:].rearrange("g (d t) -> g d t", t=GRP),
                    in_=c.d["kg"][:, :, t0:t0 + GRP]).then_inc(c.S(par), 16)
        e.dma_start(out=vt[b][:].rearrange("e (ga t) -> e ga t", ga=9)[:, 0:8, :],
                    in_=c.d["vg"].transpose([1, 0, 2])[:, :, t0:t0 + GRP]
                    ).then_inc(c.S(par), 16)
        c.bump(par, 48)
        return _mark(e)

    c.step(("gpsimd", lambda e, c: qkvload(e, c, 0)))
    c.step(("gpsimd", lambda e, c: qkvload(e, c, 1)))

    def pe_fn(e, c, k):
        r = None
        if k < NSB:
            b = (k // 4) % 3
            if k % 4 == 0:
                wdma(e, c, "w" + str((k // 4) % 2))
            ps = ps_s[k % 2]
            for tok in range(4):
                tt = (k % 4) * 4 + tok
                kap = kt[b][:].rearrange("g (e t) -> g e t", t=GRP)[:, :, tt]
                qap = qt[b][:].rearrange("g (d t) -> g d t", t=GRP)[:, :, tt]
                r = e.matmul(ps[:, tok * 128:(tok + 1) * 128], kap, qap,
                             start=True, stop=True)
        kk = k - 3
        if 0 <= kk < NSB:
            b2 = (kk // 4) % 3
            po = ps_o[(kk // 8) % 2]
            for tok in range(4):
                tt = (kk % 4) * 4 + tok
                tw = (kk * 4 + tok) % 32
                vap = vt[b2][:].rearrange("e (ga t) -> e ga t", ga=9)[:, :, tt]
                r = e.matmul(po[:, tw * 9:tw * 9 + 9],
                             expt[kk % 2][:, tok * 128:(tok + 1) * 128], vap,
                             start=True, stop=True)
        return r

    def sub_fn(e, c, k):
        kk = k - 1
        ps = ps_s[kk % 2]
        cb = (cbc[:, kk * 4:(kk + 1) * 4]
              .rearrange("p (t o) -> p t o", o=1).broadcast_to((128, 4, 128)))
        return e.tensor_tensor(subs[kk % 2][:].rearrange("p (t d) -> p t d", t=4),
                               ps[:].rearrange("p (t d) -> p t d", t=4), cb,
                               ALU.add)

    def se_fn(e, c, k):
        kk = k - 2
        return e.activation(expt[kk % 2][:], subs[kk % 2][:], AF.Exp)

    def norm_fn(e, c, k):
        w = (k - 4) // 8
        po = ps_o[w % 2]
        VG(e, c, e.reciprocal(r32[:, 0:32],
                              po[:, 0:288].rearrange("p (t n) -> p t n", t=32)[:, :, 8]),
           extra=True)
        return e.tensor_tensor(
            ot[:].rearrange("p (t g) -> p t g", t=T)[:, w * 32:(w + 1) * 32, :],
            po[:, 0:288].rearrange("p (t n) -> p t n", t=32)[:, :, 0:8],
            r32[:, 0:32].rearrange("p (t o) -> p t o", t=32).broadcast_to((128, 32, 8)),
            ALU.mult)

    for k in range(NSB + 5):
        parts = []
        if k < NSB or 0 <= k - 3 < NSB:
            parts.append(("tensor", lambda e, c, k=k: pe_fn(e, c, k)))
        if 0 <= k - 1 < NSB:
            parts.append(("vector", lambda e, c, k=k: sub_fn(e, c, k)))
        if 0 <= k - 2 < NSB:
            parts.append(("scalar", lambda e, c, k=k: se_fn(e, c, k)))
        if 0 <= k - 4 < NSB and (k - 4) % 8 == 7:
            parts.append(("vector", lambda e, c, k=k: norm_fn(e, c, k)))
        if k < NSB and k % 4 == 3 and k // 4 + 2 < NG:
            parts.append(("gpsimd", lambda e, c, k=k: qkvload(e, c, k // 4 + 2)))
        c.step(*parts)

    for blk in range(4):
        for gpair in range(4):
            def tps(e, c, blk=blk, gpair=gpair):
                r = None
                for i in range(4):
                    g8 = gpair * 2 + i // 2
                    tb = i % 2
                    ap_in = (ot[:].rearrange("p (t g) -> p t g", t=T)
                             [:, blk * 256 + tb * 128: blk * 256 + (tb + 1) * 128, g8])
                    r = e.transpose(ps_t[:, i * 128:(i + 1) * 128], ap_in, ident[:])
                return r

            c.step(("tensor", tps))

            def evt(e, c, blk=blk, gpair=gpair):
                r = None
                for i in range(4):
                    g8 = gpair * 2 + i // 2
                    tb = i % 2
                    tca = blk * 2 + tb
                    r = e.tensor_copy(
                        s0T[:, tca * 1024 + g8 * 128: tca * 1024 + (g8 + 1) * 128],
                        ps_t[:, i * 128:(i + 1) * 128])
                return r

            c.step(("vector", evt))

    def s0store(e, c):
        for tc in range(8):
            e.dma_start(out=c.d["s0"][tc * 128:(tc + 1) * 128, :],
                        in_=s0T[:, tc * 1024:(tc + 1) * 1024]).then_inc(c.S("so"), 16)
            c.bump("so", 16)
        wdma(e, c, "so")
        return _mark(e)

    c.step(("gpsimd", s0store))
    c.free("g2_qt0", "g2_qt1", "g2_qt2", "g2_kt0", "g2_kt1", "g2_kt2",
           "g2_vt0", "g2_vt1", "g2_vt2",
           "g2_ex0", "g2_ex1", "g2_sub0", "g2_sub1", "g2_crow", "g2_cbc",
           "g2_r32", "g2_ot",
           "g2_id", "g2_ones", "g2_pss0", "g2_pss1", "g2_pso0", "g2_pso1",
           "g2_pst")


def build_g3(c):
    nc = c.nc
    if not FUSED:
        load_xnT(c, "xn_bf")
    xnT = c.t["xnT"]

    wt = [c.sb(f"g3_wt{i}", [128, 4096], BF16) for i in range(4)]
    lo = c.sb("g3_lo", [16, 2 * T], BF16)
    a1 = c.sb("g3_a1", [128, NEC * 16], BF16)
    a3 = c.sb("g3_a3", [128, NEC * 16], BF16)
    b1 = c.sb("g3_b1", [16, HID], BF16)
    b3 = c.sb("g3_b3", [16, HID], BF16)
    h1s = [c.sb(f"g3_h1s{i}", [128, T], BF16) for i in range(2)]
    h3s = [c.sb(f"g3_h3s{i}", [128, T], BF16) for i in range(2)]
    gsb = [c.sb(f"g3_g{i}", [128, T], BF16) for i in range(2)]
    ps1 = [c.ps(f"g3_ps1{i}", [128, T]) for i in range(2)]
    ps3 = [c.ps(f"g3_ps3{i}", [128, T]) for i in range(2)]

    def prolog(e, c):
        e.dma_start(out=a1[:].rearrange("p (ec r) -> p ec r", r=16),
                    in_=c.d["A1T"].rearrange("(ec p) r -> p ec r", p=128)
                    ).then_inc(c.S("pl"), 16)
        e.dma_start(out=a3[:].rearrange("p (ec r) -> p ec r", r=16),
                    in_=c.d["A3T"].rearrange("(ec p) r -> p ec r", p=128)
                    ).then_inc(c.S("pl"), 16)
        e.dma_start(out=b1[:], in_=c.d["B1T"]).then_inc(c.S("pl"), 16)
        e.dma_start(out=b3[:], in_=c.d["B3T"]).then_inc(c.S("pl"), 16)
        c.bump("pl", 64)
        return _mark(e)

    def wload(e, c, hc):
        b = hc % 2
        par = "w" + str(b)
        e.dma_start(out=wt[2 * b][:].rearrange("p (ec col) -> p ec col", col=128),
                    in_=c.d["w1c"][hc].rearrange("(ec p) col -> p ec col", p=128)
                    ).then_inc(c.S(par), 16)
        e.dma_start(out=wt[2 * b + 1][:].rearrange("p (ec col) -> p ec col", col=128),
                    in_=c.d["w3c"][hc].rearrange("(ec p) col -> p ec col", p=128)
                    ).then_inc(c.S(par), 16)
        c.bump(par, 32)
        return _mark(e)

    c.step(("gpsimd", prolog))
    c.step(("gpsimd", lambda e, c: wload(e, c, 0)))

    def lomm(e, c):
        wdma(e, c, "pl")
        r = None
        for i, a in enumerate((a1, a3)):
            ps = ps1[i]
            for th in range(2):
                for ec in range(NEC):
                    r = e.matmul(ps[0:16, th * 512:(th + 1) * 512],
                                 a[:, ec * 16:(ec + 1) * 16],
                                 xnT[:, ec * T + th * 512: ec * T + (th + 1) * 512],
                                 start=(ec == 0), stop=(ec == NEC - 1))
        return r

    c.step(("tensor", lomm))

    def loev(e, c):
        VG(e, c, e.tensor_scalar(lo[:, 0:T], ps1[0][0:16, :], 2.0, None, ALU.mult))
        VG(e, c, e.tensor_scalar(lo[:, T:2 * T], ps1[1][0:16, :], 2.0, None, ALU.mult))
        return None

    c.step(("vector", loev))

    def mm_hc(e, c, hc):
        b = hc % 2
        wdma(e, c, "w" + str(b))
        r = None
        for which in range(2):
            wti = wt[2 * b + which]
            bx = b1 if which == 0 else b3
            ps = (ps1 if which == 0 else ps3)[b]
            for th in range(2):
                for ec in range(NEC):
                    e.matmul(ps[:, th * 512:(th + 1) * 512],
                             wti[:, ec * 128:(ec + 1) * 128],
                             xnT[:, ec * T + th * 512: ec * T + (th + 1) * 512],
                             start=(ec == 0), stop=False)
                r = e.matmul(ps[:, th * 512:(th + 1) * 512],
                             bx[:, hc * 128:(hc + 1) * 128],
                             lo[:, which * T + th * 512: which * T + (th + 1) * 512],
                             start=False, stop=True)
        return r

    def evacs(e, c, hc):
        return e.activation(h1s[hc % 2][:], ps1[hc % 2][:], AF.Silu)

    def evacv(e, c, hc):
        return e.tensor_copy(h3s[hc % 2][:], ps3[hc % 2][:])

    def gmult(e, c, hc):
        wdma(e, c, "s" + str(hc % 2))
        return e.tensor_tensor(gsb[hc % 2][:], h1s[hc % 2][:], h3s[hc % 2][:],
                               ALU.mult)

    def gstore(e, c, hc):
        par = "s" + str(hc % 2)
        e.dma_start(out=c.d["gdram"][hc * 128:(hc + 1) * 128, :],
                    in_=gsb[hc % 2][:]).then_inc(c.S(par), 16)
        c.bump(par, 16)
        return _mark(e)

    for i in range(NHC + 3):
        parts = []
        if i < NHC:
            parts.append(("tensor", lambda e, c, i=i: mm_hc(e, c, i)))
        if 0 <= i - 1 < NHC:
            parts.append(("scalar", lambda e, c, i=i: evacs(e, c, i - 1)))
            parts.append(("vector", lambda e, c, i=i: evacv(e, c, i - 1)))
        if 0 <= i - 2 < NHC:
            parts.append(("vector", lambda e, c, i=i: gmult(e, c, i - 2)))
        if 0 <= i - 3 < NHC:
            parts.append(("gpsimd", lambda e, c, i=i: gstore(e, c, i - 3)))
        if 0 <= i < NHC and i + 1 < NHC:
            parts.append(("gpsimd", lambda e, c, i=i: wload(e, c, i + 1)))
        c.step(*parts)

    def fin(e, c):
        wdma(e, c, "s0")
        wdma(e, c, "s1")
        return _mark(e)

    c.step(("gpsimd", fin))
    c.free("g3_wt0", "g3_wt1", "g3_wt2", "g3_wt3", "g3_lo", "g3_a1", "g3_a3",
           "g3_b1", "g3_b3", "g3_h1s0", "g3_h1s1", "g3_h3s0", "g3_h3s1",
           "g3_g0", "g3_g1", "g3_ps10", "g3_ps11", "g3_ps30", "g3_ps31")


def build_g4(c):
    nc = c.nc
    s0T = c.t["s0T"]

    if not FUSED:
        def s0load(e, c):
            for tc in range(8):
                e.dma_start(out=s0T[:, tc * 1024:(tc + 1) * 1024],
                            in_=c.d["s0"][tc * 128:(tc + 1) * 128, :]
                            ).then_inc(c.S("pl"), 16)
                c.bump("pl", 16)
            wdma(e, c, "pl")
            return _mark(e)

        c.step(("gpsimd", s0load))

    gt = [c.sb(f"g4_gt{i}", [128, 8 * T], BF16) for i in range(2)]
    a2 = c.sb("g4_a2", [128, NHC * 16], BF16)
    lo2 = c.sb("g4_lo2", [16, T], BF16)
    ps_lo = c.ps("g4_plo", [16, T])

    def prolog(e, c):
        e.dma_start(out=a2[:].rearrange("p (hc r) -> p hc r", r=16),
                    in_=c.d["A2T"].rearrange("(hc p) r -> p hc r", p=128)
                    ).then_inc(c.S("pl"), 16)
        c.bump("pl", 16)
        return _mark(e)

    NSL = (NHC + 7) // 8

    def gload(e, c, sl, slot):
        par = "a" + str(slot)
        n = min(8, NHC - sl * 8)
        e.dma_start(out=gt[slot][:].rearrange("p (cc n) -> p cc n", n=T)[:, 0:n, :],
                    in_=c.d["gdram"][sl * 1024:sl * 1024 + n * 128, :]
                    .rearrange("(cc p) n -> p cc n", p=128)).then_inc(c.S(par), 16)
        c.bump(par, 16)
        return _mark(e)

    c.step(("gpsimd", prolog))
    c.step(("sync", lambda e, c: gload(e, c, 0, 0)))
    for sl in range(NSL):
        def mm(e, c, sl=sl):
            if sl == 0:
                wdma(e, c, "pl")
            wdma(e, c, "a" + str(sl % 2))
            r = None
            for j in range(min(8, NHC - sl * 8)):
                hc = sl * 8 + j
                for th in range(2):
                    r = e.matmul(ps_lo[0:16, th * 512:(th + 1) * 512],
                                 a2[:, hc * 16:(hc + 1) * 16],
                                 gt[sl % 2][:, j * T + th * 512: j * T + (th + 1) * 512],
                                 start=(hc == 0), stop=(hc == NHC - 1))
            return r

        parts = [("tensor", mm)]
        if sl + 1 < NSL:
            parts.append(("sync", lambda e, c, sl=sl: gload(e, c, sl + 1, (sl + 1) % 2)))
        c.step(*parts)

    c.step(("vector", lambda e, c: VG(e, c, e.tensor_scalar(lo2[:], ps_lo[0:16, :],
                                                             2.0, None, ALU.mult)) and None))

    def lostore(e, c):
        e.dma_start(out=c.d["h2lo_d"], in_=lo2[:]).then_inc(c.S("so"), 16)
        c.bump("so", 16)
        return _mark(e)

    c.step(("gpsimd", lostore))
    c.free("g4_plo")

    w2t = [c.sb(f"g4_w2t{i}", [128, 1024], BF16) for i in range(3)]
    gt3 = [c.sb(f"g4_gt3{i}", [128, 2 * T], BF16) for i in range(3)]
    b2t = c.sb("g4_b2t", [16, EMB], BF16)
    silu_t = [c.sb(f"g4_silu{i}", [128, 512], BF16) for i in range(2)]
    sg = c.sb("g4_sg", [128, 512], F32)
    xts = [c.sb(f"g4_xt{i}", [128, 512], F32) for i in range(3)]
    xnew_t = [c.sb(f"g4_xnew{i}", [128, 512], F32) for i in range(2)]
    ps8 = [c.ps(f"g4_ps{i}", [128, 512]) for i in range(8)]

    def prolog2(e, c):
        e.dma_start(out=b2t[:], in_=c.d["B2T"]).then_inc(c.S("pl"), 16)
        c.bump("pl", 16)
        return _mark(e)

    NPAIR = NHC // 2

    def w2load(e, c, j):
        es, hp = j // NPAIR, j % NPAIR
        slot = j % 3
        par = "w" + str(j % 2)
        e.dma_start(out=w2t[slot][:].rearrange("p (cc n) -> p cc n", n=512),
                    in_=c.d["w2s"][es, hp * 256:(hp + 1) * 256, :]
                    .rearrange("(cc p) n -> p cc n", p=128)).then_inc(c.S(par), 16)
        c.bump(par, 16)
        return _mark(e)

    def gpload(e, c, j, first):
        es, hp = j // NPAIR, j % NPAIR
        slot = j % 3
        par = "a" + str(j % 2)
        e.dma_start(out=gt3[slot][:].rearrange("p (cc n) -> p cc n", n=T),
                    in_=c.d["gdram"][hp * 256:(hp + 1) * 256, :]
                    .rearrange("(cc p) n -> p cc n", p=128)).then_inc(c.S(par), 16)
        c.bump(par, 16)
        return _mark(e)

    c.step(("gpsimd", prolog2))
    c.step(("gpsimd", lambda e, c: w2load(e, c, 0)),
           ("sync", lambda e, c: gpload(e, c, 0, True)))
    c.step(("gpsimd", lambda e, c: w2load(e, c, 1)),
           ("sync", lambda e, c: gpload(e, c, 1, False)))
    for es in range(8):
        for hp in range(NPAIR):
            j = es * NPAIR + hp

            def mm(e, c, j=j, es=es, hp=hp):
                if j == 0:
                    wdma(e, c, "pl")
                slot = j % 3
                wdma(e, c, "w" + str(j % 2))
                wdma(e, c, "a" + str(j % 2))
                r = None
                for half in range(2):
                    hc = hp * 2 + half
                    for tc in range(8):
                        r = e.matmul(ps8[tc][:],
                                     gt3[slot][:, half * T + tc * 128: half * T + (tc + 1) * 128],
                                     w2t[slot][:, half * 512:(half + 1) * 512],
                                     start=(hc == 0), stop=False)
                return r

            parts = [("tensor", mm)]
            if j + 2 < 8 * NPAIR:
                parts.append(("gpsimd", lambda e, c, j=j: w2load(e, c, j + 2)))
                parts.append(("sync", lambda e, c, j=j: gpload(e, c, j + 2, False)))
            c.step(*parts)

        def b2mm(e, c, es=es):
            r = None
            for tc in range(8):
                r = e.matmul(ps8[tc][:], lo2[:, tc * 128:(tc + 1) * 128],
                             b2t[:, es * 512:(es + 1) * 512], start=False, stop=True)
            return r

        c.step(("tensor", b2mm))

        def xload(e, c, es, tc):
            par = "x" + str(tc % 2)
            e.dma_start(out=xts[tc % 3][:],
                        in_=c.d["x"][tc * 128:(tc + 1) * 128,
                                     es * 512:(es + 1) * 512]).then_inc(c.S(par), 16)
            c.bump(par, 16)
            return _mark(e)

        def sil(e, c, tc):
            return e.activation(silu_t[tc % 2][:], ps8[tc][:], AF.Silu)

        def ep(e, c, es, tc):
            wdma(e, c, "x" + str(tc % 2))
            sap = (s0T[:, tc * 1024:(tc + 1) * 1024]
                   .rearrange("p (g d) -> p g d", g=8)
                   [:, :, es * 16:(es + 1) * 16]
                   .transpose([0, 2, 1])
                   .rearrange("p d (g o) -> p d g o", o=1)
                   .broadcast_to((128, 16, 8, 4)))
            e.tensor_tensor(sg[:].rearrange("p (d g j) -> p d g j", d=16, g=8),
                            silu_t[tc % 2][:].rearrange("p (d g j) -> p d g j",
                                                        d=16, g=8),
                            sap, ALU.mult)
            wdma(e, c, "s" + str(tc % 2))
            return e.tensor_tensor(xnew_t[tc % 2][:], xts[tc % 3][:], sg[:], ALU.add)

        def nstore(e, c, es, tc):
            par = "s" + str(tc % 2)
            e.dma_start(out=c.d["xnew_d"][tc * 128:(tc + 1) * 128,
                                          es * 512:(es + 1) * 512],
                        in_=xnew_t[tc % 2][:]).then_inc(c.S(par), 16)
            c.bump(par, 16)
            return _mark(e)

        c.step(("sync", lambda e, c, es=es: xload(e, c, es, 0)))
        c.step(("sync", lambda e, c, es=es: xload(e, c, es, 1)))
        for tc in range(10):
            parts = []
            if tc < 8:
                parts.append(("scalar", lambda e, c, tc=tc: sil(e, c, tc)))
                if 2 <= tc + 1 < 8:
                    parts.append(("sync", lambda e, c, es=es, tc=tc:
                                  xload(e, c, es, tc + 1)))
            if 0 <= tc - 1 < 8:
                parts.append(("vector", lambda e, c, es=es, tc=tc: ep(e, c, es, tc - 1)))
            if 0 <= tc - 2 < 8:
                parts.append(("gpsimd", lambda e, c, es=es, tc=tc: nstore(e, c, es, tc - 2)))
            c.step(*parts)

    def fin(e, c):
        wdma(e, c, "s0")
        wdma(e, c, "s1")
        wdma(e, c, "so")
        return _mark(e)

    c.step(("gpsimd", fin))
    c.free("g4_gt0", "g4_gt1", "g4_a2", "g4_lo2", "g4_w2t0", "g4_w2t1", "g4_w2t2",
           "g4_gt30", "g4_gt31", "g4_gt32",
           "g4_b2t", "g4_silu0", "g4_silu1", "g4_sg", "g4_xt0", "g4_xt1",
           "g4_xt2", "g4_xnew0", "g4_xnew1", *(f"g4_ps{i}" for i in range(8)))
    c.free("s0T")
    del c.t["s0T"]


def build_g5(c):
    nc = c.nc
    ln_global(c, "xnew_d", "xn1_bf", "l1")
    load_xnT(c, "xn1_bf")
    xnT = c.t["xnT"]

    ident = c.sb("g5_id", [128, 128], BF16)
    identF = c.sb("g5_idF", [128, 128], F32)
    onesc = c.sb("g5_onesc", [128, 1], BF16)
    onesr = c.sb("g5_onesr", [1, 128], BF16)
    is2 = c.sb("g5_is2", [128, 1], F32)
    isA = c.sb("g5_isA", [128, 1], F32)
    boutr = c.sb("g5_boutr", [1, EMB], BF16)
    id_d = nc.inline_tensor(np.eye(128, dtype=BF), name="g5ident").ap()
    idF_d = nc.inline_tensor(np.eye(128, dtype=np.float32), name="g5identF").ap()
    oc_d = nc.inline_tensor(np.ones((128, 1), dtype=BF), name="g5onesc").ap()
    or_d = nc.inline_tensor(np.ones((1, 128), dtype=BF), name="g5onesr").ap()

    pnames = ["ln1_w", "ln1_b", "phz_re", "phz_im", "lc_re", "lc_im",
              "sc_lin", "ln2_w", "ln2_b", "b2", "bin", "lnl_w", "lnl_b", "b1"]
    prows = {"b2": 16, "bin": 4, "lnl_w": 4, "lnl_b": 4, "b1": 64}
    pstage = c.sb("g5_pstage", [128, 128], F32)
    ppar = {nm: c.sb("g5_pp_" + nm, [128, prows.get(nm, 16)], F32) for nm in pnames}
    ps_pt = c.ps("g5_pspt", [128, 128])

    def prolog(e, c):
        for src, dst in ((id_d, ident), (idF_d, identF), (oc_d, onesc),
                         (or_d, onesr), (c.d["is2"], is2),
                         (c.d["bout_row"], boutr)):
            e.dma_start(out=dst[:], in_=src).then_inc(c.S("pl"), 16)
            c.bump("pl", 16)
        return _mark(e)

    c.step(("gpsimd", prolog))

    def isAset(e, c):
        wdma(e, c, "pl")
        VG(e, c, e.memset(isA[:], 1.0))
        VG(e, c, e.tensor_tensor(isA[:], isA[:], is2[:], ALU.subtract))
        return None

    c.step(("vector", isAset))

    for nm in pnames:
        rows = prows.get(nm, 16)

        def pload(e, c, nm=nm, rows=rows):
            e.dma_start(out=pstage[0:rows, :], in_=c.d["p_" + nm]).then_inc(c.S("pl"), 16)
            c.bump("pl", 16)
            return _mark(e)

        c.step(("gpsimd", pload))

        def ptp(e, c, rows=rows):
            wdma(e, c, "pl")
            return e.transpose(ps_pt[:, 0:rows], pstage[0:rows, :],
                               identF[0:rows, 0:rows])

        c.step(("tensor", ptp))
        c.step(("vector", lambda e, c, nm=nm, rows=rows:
                VG(e, c, e.tensor_copy(ppar[nm][:, 0:rows], ps_pt[:, 0:rows])) and None))

    h = c.sb("g5_h", [128, NDC * T], F32)
    winb = [c.sb(f"g5_winb{i}", [128, 4096], BF16) for i in range(2)]
    ps_a = c.ps("g5_psa", [128, 512])
    ps_b = c.ps("g5_psb", [128, 512])

    def winload(e, c, dc):
        par = "w" + str(dc % 2)
        e.dma_start(out=winb[dc % 2][:].rearrange("p (ec col) -> p ec col", col=128),
                    in_=c.d["WinT"].rearrange("(ec p) d -> p ec d", p=128)
                    [:, :, dc * 128:(dc + 1) * 128]).then_inc(c.S(par), 16)
        c.bump(par, 16)
        return _mark(e)

    c.step(("gpsimd", lambda e, c: winload(e, c, 0)))
    for dc in range(NDC):
        def wmm(e, c, dc=dc):
            wdma(e, c, "w" + str(dc % 2))
            r = None
            for th in range(2):
                ps = ps_a if th == 0 else ps_b
                for ec in range(NEC):
                    r = e.matmul(ps[:], winb[dc % 2][:, ec * 128:(ec + 1) * 128],
                                 xnT[:, ec * T + th * 512: ec * T + (th + 1) * 512],
                                 start=(ec == 0), stop=(ec == NEC - 1))
            return r

        parts = [("tensor", wmm)]
        if dc + 1 < NDC:
            parts.append(("gpsimd", lambda e, c, dc=dc: winload(e, c, dc + 1)))
        c.step(*parts)

        def wev(e, c, dc=dc):
            e.tensor_scalar(h[:, dc * T: dc * T + 512], ps_a[:],
                            ppar["bin"][:, dc:dc + 1], None, ALU.add)
            return e.tensor_scalar(h[:, dc * T + 512: dc * T + 1024], ps_b[:],
                                   ppar["bin"][:, dc:dc + 1], None, ALU.add)

        c.step(("vector", wev))

    def hdump(slot):
        if not DEBUG_G5:
            return

        def d(e, c, slot=slot):
            e.dma_start(out=c.d["h_dbg"][:, slot * NDC * T:(slot + 1) * NDC * T],
                        in_=h[:]).then_inc(c.S("hdbg"), 16)
            c.bump("hdbg", 16)
            wdma(e, c, "hdbg")
            return _mark(e)

        c.step(("gpsimd", d))

    hdump(0)
    c.free("g5_winb0", "g5_winb1")
    c.free("xnT")
    del c.t["xnT"]

    hb = c.sb("g5_hb", [128, NDC * T], BF16)
    urho = c.sb("g5_urho", [128, 28], F32)
    dini = c.sb("g5_dini", [128, 16], F32)
    tails = c.sb("g5_tails", [128, 8], F32)
    recv = c.sb("g5_recv", [128, 8], F32)
    row = c.sb("g5_row", [1, 4 * T], F32)
    rowb = c.sb("g5_rowb", [1, 2 * T], BF16)
    zero1 = c.sb("g5_zero", [128, 1], F32)
    hsq = c.sb("g5_hsq", [128, 512], BF16)
    yv = c.sb("g5_yv", [128, T], F32)
    Tcb = c.sb("g5_Tc", [128, NDC * T], BF16)
    Tsb = c.sb("g5_Ts", [128, NDC * T], BF16)
    Tf32c = c.sb("g5_Tfc", [128, T], F32)
    Tf32s = c.sb("g5_Tfs", [128, T], F32)
    ytr = c.sb("g5_ytr", [128, T], F32)
    yti = c.sb("g5_yti", [128, T], F32)
    dr = c.sb("g5_dr", [128, NDC * T], BF16)
    di = c.sb("g5_di", [128, NDC * T], BF16)
    yh = c.sb("g5_yh", [128, NFC * T], BF16)
    w1t = [c.sb(f"g5_w1t{i}", [128, 512], BF16) for i in range(3)]
    w2t = [c.sb(f"g5_w2t{i}", [128, 2048], BF16) for i in range(2)]
    ps_bc1 = c.ps("g5_psbc1", [128, T])
    ps_bc2 = c.ps("g5_psbc2", [128, T])
    cc_in = [nc.dram_tensor(f"cc_in{l}", [128, 8], F32) for l in range(DEPTH1)]
    cc_out = [nc.dram_tensor(f"cc_out{l}", [128, 8], F32) for l in range(DEPTH1)]

    c.step(("vector", lambda e, c: e.memset(zero1[:], 0.0)))

    def feat_stats(c):
        c.step(("vector", lambda e, c: e.tensor_copy(hb[:], h[:])))

        def summ(e, c):
            r = None
            for th in range(2):
                ps = ps_a if th == 0 else ps_b
                for dc in range(NDC):
                    r = e.matmul(ps[0:1, :], onesc[:],
                                 hb[:, dc * T + th * 512: dc * T + (th + 1) * 512],
                                 start=(dc == 0), stop=(dc == NDC - 1))
            return r

        c.step(("tensor", summ))

        def sev(e, c):
            VG(e, c, e.tensor_copy(row[:, 0:512], ps_a[0:1, :]))
            VG(e, c, e.tensor_copy(row[:, 512:1024], ps_b[0:1, :]))
            return None

        c.step(("vector", sev))
        for th in range(2):
            for dc in range(NDC):
                c.step(("scalar", lambda e, c, th=th, dc=dc:
                        e.activation(hsq[:],
                                     hb[:, dc * T + th * 512: dc * T + (th + 1) * 512],
                                     AF.Square)))
                c.step(("tensor", lambda e, c, th=th, dc=dc:
                        e.matmul((ps_a if th == 0 else ps_b)[0:1, :], onesc[:],
                                 hsq[:, 0:512], start=(dc == 0),
                                 stop=(dc == NDC - 1))))

        def stats(e, c):
            VG(e, c, e.tensor_copy(row[:, T:T + 512], ps_a[0:1, :]))
            VG(e, c, e.tensor_copy(row[:, T + 512:2 * T], ps_b[0:1, :]))
            VG(e, c, e.tensor_scalar(row[:, 2 * T:3 * T], row[:, 0:T], -1.0 / D1,
                                     None, ALU.mult))
            VG(e, c, e.tensor_scalar(row[:, T:2 * T], row[:, T:2 * T], 1.0 / D1,
                                     None, ALU.mult))
            VG(e, c, e.tensor_tensor(row[:, 3 * T:4 * T], row[:, 2 * T:3 * T],
                                     row[:, 2 * T:3 * T], ALU.mult))
            VG(e, c, e.tensor_tensor(row[:, T:2 * T], row[:, T:2 * T],
                                     row[:, 3 * T:4 * T], ALU.subtract))
            VG(e, c, e.tensor_scalar(row[:, T:2 * T], row[:, T:2 * T], EPS,
                                     None, ALU.add))
            return None

        c.step(("vector", stats))
        c.step(("scalar", lambda e, c: VG(e, c, e.activation(row[:, 3 * T:4 * T],
                                                              row[:, T:2 * T], AF.Sqrt)) and None))

        def rstd(e, c):
            VG(e, c, e.reciprocal(row[:, T:2 * T], row[:, 3 * T:4 * T]), extra=True)
            VG(e, c, e.tensor_copy(rowb[:, 0:T], row[:, 2 * T:3 * T]))
            VG(e, c, e.tensor_copy(rowb[:, T:2 * T], row[:, T:2 * T]))
            return None

        c.step(("vector", rstd))

        def bcast(e, c):
            r = None
            for th in range(2):
                e.matmul(ps_bc1[:, th * 512:(th + 1) * 512], onesr[:],
                         rowb[:, th * 512:(th + 1) * 512], start=True, stop=True)
                r = e.matmul(ps_bc2[:, th * 512:(th + 1) * 512], onesr[:],
                             rowb[:, T + th * 512:T + (th + 1) * 512],
                             start=True, stop=True)
            return r

        c.step(("tensor", bcast))

    def sdump(slot, tile, l, dc, width=T):
        if not DEBUG_G5 or l != 0 or dc != 0:
            return

        def d(e, c, slot=slot, tile=tile, width=width):
            e.dma_start(out=c.d["s_dbg"][:, slot * T:slot * T + width],
                        in_=tile).then_inc(c.S("hdbg"), 16)
            c.bump("hdbg", 16)
            wdma(e, c, "hdbg")
            return _mark(e)

        c.step(("gpsimd", d))

    for l in range(DEPTH1):
        feat_stats(c)

        def uprep(e, c, l=l):
            pr = ppar["phz_re"][:, l * 4:(l + 1) * 4]
            pi = ppar["phz_im"][:, l * 4:(l + 1) * 4]
            VG(e, c, e.tensor_tensor(urho[:, 0:4], pr, pr, ALU.mult))
            VG(e, c, e.tensor_tensor(urho[:, 4:8], pi, pi, ALU.mult))
            VG(e, c, e.tensor_tensor(urho[:, 0:4], urho[:, 0:4], urho[:, 4:8], ALU.add))
            return None

        c.step(("vector", uprep))
        c.step(("scalar", lambda e, c: VG(e, c, e.activation(urho[:, 4:8],
                                                              urho[:, 0:4], AF.Sqrt)) and None))

        def u2(e, c, l=l):
            pr = ppar["phz_re"][:, l * 4:(l + 1) * 4]
            pi = ppar["phz_im"][:, l * 4:(l + 1) * 4]
            VG(e, c, e.reciprocal(urho[:, 8:12], urho[:, 4:8]), extra=True)
            VG(e, c, e.tensor_tensor(urho[:, 12:16], pr, urho[:, 8:12], ALU.mult))
            VG(e, c, e.tensor_tensor(urho[:, 16:20], pi, urho[:, 8:12], ALU.mult))
            return None

        c.step(("vector", u2))
        c.step(("scalar", lambda e, c: VG(e, c, e.activation(urho[:, 20:24],
                                                             urho[:, 4:8], AF.Tanh)) and None))

        def dini_intra(e, c, l=l):
            lcr = ppar["lc_re"][:, l * 4:(l + 1) * 4]
            lci = ppar["lc_im"][:, l * 4:(l + 1) * 4]
            ur, ui = urho[:, 12:16], urho[:, 16:20]
            VG(e, c, e.tensor_tensor(dini[:, 0:4], lcr, ur, ALU.mult))
            VG(e, c, e.tensor_tensor(urho[:, 24:28], lci, ui, ALU.mult))
            VG(e, c, e.tensor_tensor(dini[:, 0:4], dini[:, 0:4], urho[:, 24:28],
                                     ALU.subtract))
            VG(e, c, e.tensor_tensor(dini[:, 4:8], lcr, ui, ALU.mult))
            VG(e, c, e.tensor_tensor(urho[:, 24:28], lci, ur, ALU.mult))
            VG(e, c, e.tensor_tensor(dini[:, 4:8], dini[:, 4:8], urho[:, 24:28],
                                     ALU.add))
            VG(e, c, e.tensor_tensor(dini[:, 0:4], dini[:, 0:4],
                                     isA[:, 0:1].broadcast_to((128, 4)), ALU.mult))
            VG(e, c, e.tensor_tensor(dini[:, 4:8], dini[:, 4:8],
                                     isA[:, 0:1].broadcast_to((128, 4)), ALU.mult))
            return None

        c.step(("vector", dini_intra))

        for dc in range(NDC):
            def ybuild(e, c, l=l, dc=dc):
                wcol = ppar["ln1_w"][:, l * 4 + dc:l * 4 + dc + 1]
                bcol = ppar["ln1_b"][:, l * 4 + dc:l * 4 + dc + 1]
                e.tensor_tensor(yv[:], h[:, dc * T:(dc + 1) * T], ps_bc1[:], ALU.add)
                e.tensor_tensor(yv[:], yv[:], ps_bc2[:], ALU.mult)
                return e.tensor_scalar(yv[:], yv[:], wcol, bcol, ALU.mult, ALU.add)

            c.step(("vector", ybuild))
            c.step(("scalar", lambda e, c: e.activation(yv[:], yv[:], AF.Silu)))
            sdump(0, yv[:], l, dc)

            def tbl0(e, c, dc=dc):
                VG(e, c, e.memset(Tf32c[:, 0:1], 1.0))
                VG(e, c, e.memset(Tf32s[:, 0:1], 0.0))
                VG(e, c, e.tensor_copy(urho[:, 24:25], urho[:, 12 + dc:13 + dc]))
                VG(e, c, e.tensor_copy(urho[:, 25:26], urho[:, 16 + dc:17 + dc]))
                return None

            c.step(("vector", tbl0))
            for kk in range(10):
                n = 1 << kk

                def dbl(e, c, dc=dc, kk=kk, n=n):
                    cn = urho[:, 24:25]
                    sn = urho[:, 25:26]
                    A = Tf32c[:, 0:n]
                    B = Tf32s[:, 0:n]
                    VG(e, c, e.tensor_scalar(ytr[:, 0:n], B, sn, None, ALU.mult))
                    VG(e, c, e.tensor_scalar(yti[:, 0:n], A, sn, None, ALU.mult))
                    VG(e, c, e.scalar_tensor_tensor(Tf32c[:, n:2 * n], A, cn,
                                                    ytr[:, 0:n], ALU.mult,
                                                    ALU.subtract))
                    VG(e, c, e.scalar_tensor_tensor(Tf32s[:, n:2 * n], B, cn,
                                                    yti[:, 0:n], ALU.mult, ALU.add))
                    if kk < 9:
                        VG(e, c, e.tensor_tensor(urho[:, 26:27], cn, cn, ALU.mult))
                        VG(e, c, e.tensor_tensor(urho[:, 27:28], sn, sn, ALU.mult))
                        VG(e, c, e.tensor_tensor(urho[:, 27:28], urho[:, 26:27],
                                                 urho[:, 27:28], ALU.subtract))
                        VG(e, c, e.tensor_tensor(urho[:, 26:27], cn, sn, ALU.mult))
                        VG(e, c, e.tensor_copy(urho[:, 24:25], urho[:, 27:28]))
                        VG(e, c, e.tensor_scalar(urho[:, 25:26], urho[:, 26:27],
                                                 2.0, None, ALU.mult))
                    return None

                c.step(("vector", dbl))

            def tblcast(e, c, dc=dc):
                base = dc * T
                e.tensor_copy(Tcb[:, base:base + T], Tf32c[:])
                return e.tensor_copy(Tsb[:, base:base + T], Tf32s[:])

            c.step(("vector", tblcast))
            sdump(1, Tf32c[:], l, dc)
            sdump(2, Tf32s[:], l, dc)

            def ytilde(e, c, dc=dc):
                e.tensor_tensor(ytr[:], yv[:], Tf32c[:], ALU.mult)
                return e.scalar_tensor_tensor(yti[:], yv[:], -1.0,
                                              Tf32s[:], ALU.mult, ALU.mult)

            c.step(("vector", ytilde))

            def scans(e, c, dc=dc):
                base = dc * T
                rho = urho[:, 20 + dc:21 + dc].broadcast_to((128, T))
                VG(e, c, e.tensor_tensor_scan(dr[:, base:base + T], rho, ytr[:],
                                              dini[:, dc:dc + 1], ALU.mult, ALU.add),
                   extra=True)
                VG(e, c, e.tensor_tensor_scan(di[:, base:base + T], rho, yti[:],
                                              dini[:, 4 + dc:5 + dc], ALU.mult,
                                              ALU.add), extra=True)
                return None

            c.step(("vector", scans))
            sdump(3, dr[:, 0:T], l, dc)
            sdump(4, di[:, 0:T], l, dc)

            def tail(e, c, dc=dc):
                base = dc * T
                VG(e, c, e.tensor_tensor(tails[:, dc:dc + 1],
                                         dr[:, base + T - 1:base + T],
                                         Tcb[:, base + T - 1:base + T], ALU.mult))
                VG(e, c, e.tensor_tensor(urho[:, 24:25],
                                         di[:, base + T - 1:base + T],
                                         Tsb[:, base + T - 1:base + T], ALU.mult))
                VG(e, c, e.tensor_tensor(tails[:, dc:dc + 1], tails[:, dc:dc + 1],
                                         urho[:, 24:25], ALU.subtract))
                VG(e, c, e.tensor_tensor(tails[:, 4 + dc:5 + dc],
                                         dr[:, base + T - 1:base + T],
                                         Tsb[:, base + T - 1:base + T], ALU.mult))
                VG(e, c, e.tensor_tensor(urho[:, 24:25],
                                         di[:, base + T - 1:base + T],
                                         Tcb[:, base + T - 1:base + T], ALU.mult))
                VG(e, c, e.tensor_tensor(tails[:, 4 + dc:5 + dc],
                                         tails[:, 4 + dc:5 + dc],
                                         urho[:, 24:25], ALU.add))
                return None

            c.step(("vector", tail))

        def ccgo(e, c, l=l):
            e.dma_start(out=cc_in[l].ap(), in_=tails[:]).then_inc(c.S("cc"), 16)
            c.bump("cc", 16)
            wdma(e, c, "cc")
            e.collective_compute("AllReduce", ALU.add,
                                 replica_groups=[[0, 1], [2, 3], [4, 5], [6, 7]],
                                 ins=[cc_in[l].ap().opt()],
                                 outs=[cc_out[l].ap().opt()]).then_inc(c.S("cd"), 1)
            c.bump("cd", 1)
            wdma(e, c, "cd")
            e.dma_start(out=recv[:], in_=cc_out[l].ap()).then_inc(c.S("cr"), 16)
            c.bump("cr", 16)
            wdma(e, c, "cr")
            return _mark(e)

        c.step(("gpsimd", ccgo))

        def dini_corr(e, c, l=l):
            ur, ui = urho[:, 12:16], urho[:, 16:20]
            VG(e, c, e.tensor_tensor(recv[:], recv[:], tails[:], ALU.subtract))
            VG(e, c, e.tensor_tensor(dini[:, 8:12], recv[:, 0:4], ur, ALU.mult))
            VG(e, c, e.tensor_tensor(urho[:, 24:28], recv[:, 4:8], ui, ALU.mult))
            VG(e, c, e.tensor_tensor(dini[:, 8:12], dini[:, 8:12], urho[:, 24:28],
                                     ALU.subtract))
            VG(e, c, e.tensor_tensor(dini[:, 12:16], recv[:, 0:4], ui, ALU.mult))
            VG(e, c, e.tensor_tensor(urho[:, 24:28], recv[:, 4:8], ur, ALU.mult))
            VG(e, c, e.tensor_tensor(dini[:, 12:16], dini[:, 12:16], urho[:, 24:28],
                                     ALU.add))
            VG(e, c, e.tensor_tensor(dini[:, 8:12], dini[:, 8:12],
                                     is2[:, 0:1].broadcast_to((128, 4)), ALU.mult))
            VG(e, c, e.tensor_tensor(dini[:, 12:16], dini[:, 12:16],
                                     is2[:, 0:1].broadcast_to((128, 4)), ALU.mult))
            return None

        c.step(("vector", dini_corr))

        for dc in range(NDC):
            def corr(e, c, dc=dc):
                base = dc * T
                rho = urho[:, 20 + dc:21 + dc].broadcast_to((128, T))
                z = zero1[:, 0:1].broadcast_to((128, T))
                VG(e, c, e.tensor_tensor_scan(ytr[:], rho, z, dini[:, 8 + dc:9 + dc],
                                              ALU.mult, ALU.add), extra=True)
                VG(e, c, e.tensor_tensor_scan(yti[:], rho, z, dini[:, 12 + dc:13 + dc],
                                              ALU.mult, ALU.add), extra=True)
                e.tensor_tensor(ytr[:], ytr[:], dr[:, base:base + T], ALU.add)
                return e.tensor_tensor(yti[:], yti[:], di[:, base:base + T], ALU.add)

            c.step(("vector", corr))

            def comb(e, c, l=l, dc=dc):
                base = dc * T
                scl = ppar["sc_lin"][:, l * 4 + dc:l * 4 + dc + 1]
                e.tensor_tensor(ytr[:], ytr[:], Tcb[:, base:base + T], ALU.mult)
                e.tensor_tensor(yti[:], yti[:], Tsb[:, base:base + T], ALU.mult)
                e.tensor_tensor(ytr[:], ytr[:], yti[:], ALU.subtract)
                return e.scalar_tensor_tensor(h[:, base:base + T], ytr[:], scl,
                                              h[:, base:base + T], ALU.mult, ALU.add)

            c.step(("vector", comb))
            sdump(5, ytr[:], l, dc)

        hdump(1 + 2 * l)
        feat_stats(c)
        for dc in range(NDC):
            def yb2(e, c, l=l, dc=dc):
                wcol = ppar["ln2_w"][:, l * 4 + dc:l * 4 + dc + 1]
                bcol = ppar["ln2_b"][:, l * 4 + dc:l * 4 + dc + 1]
                e.tensor_tensor(yv[:], h[:, dc * T:(dc + 1) * T], ps_bc1[:], ALU.add)
                e.tensor_tensor(yv[:], yv[:], ps_bc2[:], ALU.mult)
                return e.tensor_scalar(hb[:, dc * T:(dc + 1) * T], yv[:], wcol, bcol,
                                       ALU.mult, ALU.add)

            c.step(("vector", yb2))

        def w1load(e, c, l, fc):
            par = "w" + str(fc % 2)
            e.dma_start(out=w1t[fc % 3][:].rearrange("p (dc f) -> p dc f", f=128),
                        in_=c.d["W1T4"][l].rearrange("(dc p) f -> p dc f", p=128)
                        [:, :, fc * 128:(fc + 1) * 128]).then_inc(c.S(par), 16)
            c.bump(par, 16)
            return _mark(e)

        c.step(("gpsimd", lambda e, c, l=l: w1load(e, c, l, 0)))
        c.step(("gpsimd", lambda e, c, l=l: w1load(e, c, l, 1)))
        for fc in range(NFC + 1):
            parts = []
            if fc < NFC:
                def fmm(e, c, l=l, fc=fc):
                    wdma(e, c, "w" + str(fc % 2))
                    r = None
                    ps = ps_bc1 if fc % 2 == 0 else ps_bc2
                    for th in range(2):
                        for dc in range(NDC):
                            r = e.matmul(ps[:, th * 512:(th + 1) * 512],
                                         w1t[fc % 3][:, dc * 128:(dc + 1) * 128],
                                         hb[:, dc * T + th * 512: dc * T + (th + 1) * 512],
                                         start=(dc == 0), stop=(dc == NDC - 1))
                    return r

                parts.append(("tensor", fmm))
                if fc + 2 < NFC:
                    parts.append(("gpsimd", lambda e, c, l=l, fc=fc:
                                  w1load(e, c, l, fc + 2)))
            if fc >= 1:
                def fev(e, c, l=l, fc=fc):
                    ps = ps_bc1 if (fc - 1) % 2 == 0 else ps_bc2
                    return e.activation(yh[:, (fc - 1) * T: fc * T], ps[:], AF.Silu,
                                        bias=ppar["b1"][:, l * 16 + fc - 1: l * 16 + fc])

                parts.append(("scalar", fev))
            c.step(*parts)

        def w2load(e, c, l, dc):
            par = "w" + str(dc % 2)
            e.dma_start(out=w2t[dc % 2][:].rearrange("p (fc d) -> p fc d", d=128),
                        in_=c.d["W2T4"][l].rearrange("(fc p) d -> p fc d", p=128)
                        [:, :, dc * 128:(dc + 1) * 128]).then_inc(c.S(par), 16)
            c.bump(par, 16)
            return _mark(e)

        c.step(("gpsimd", lambda e, c, l=l: w2load(e, c, l, 0)))
        for dc in range(NDC + 1):
            parts = []
            if dc < NDC:
                def f2mm(e, c, l=l, dc=dc):
                    wdma(e, c, "w" + str(dc % 2))
                    r = None
                    ps = ps_bc1 if dc % 2 == 0 else ps_bc2
                    for th in range(2):
                        for fc in range(NFC):
                            r = e.matmul(ps[:, th * 512:(th + 1) * 512],
                                         w2t[dc % 2][:, fc * 128:(fc + 1) * 128],
                                         yh[:, fc * T + th * 512: fc * T + (th + 1) * 512],
                                         start=(fc == 0), stop=(fc == NFC - 1))
                    return r

                parts.append(("tensor", f2mm))
                if dc + 1 < NDC:
                    parts.append(("gpsimd", lambda e, c, l=l, dc=dc:
                                  w2load(e, c, l, dc + 1)))
            if dc >= 1:
                def f2ev(e, c, l=l, dc=dc):
                    ps = ps_bc1 if (dc - 1) % 2 == 0 else ps_bc2
                    b2c = ppar["b2"][:, l * 4 + dc - 1: l * 4 + dc]
                    return e.scalar_tensor_tensor(h[:, (dc - 1) * T: dc * T], ps[:],
                                                  b2c, h[:, (dc - 1) * T: dc * T],
                                                  ALU.add, ALU.add)

                parts.append(("vector", f2ev))
            c.step(*parts)

        hdump(2 + 2 * l)
    feat_stats(c)
    for dc in range(NDC):
        def ybl(e, c, dc=dc):
            wcol = ppar["lnl_w"][:, dc:dc + 1]
            bcol = ppar["lnl_b"][:, dc:dc + 1]
            e.tensor_tensor(yv[:], h[:, dc * T:(dc + 1) * T], ps_bc1[:], ALU.add)
            e.tensor_tensor(yv[:], yv[:], ps_bc2[:], ALU.mult)
            return e.tensor_scalar(hb[:, dc * T:(dc + 1) * T], yv[:], wcol, bcol,
                                   ALU.mult, ALU.add)

        c.step(("vector", ybl))

    c.free("g5_Tc", "g5_Ts", "g5_Tfc", "g5_Tfs", "g5_ytr", "g5_yti",
           "g5_dr", "g5_di", "g5_yv", "g5_hsq", "g5_yh",
           "g5_w1t0", "g5_w1t1", "g5_w1t2", "g5_w2t0", "g5_w2t1")

    s1 = c.t["s1"] = c.sb("s1", [128, 8 * EMB], BF16, side="right")
    wot = [c.sb(f"g5_wot{i}", [128, 2048], BF16) for i in range(2)]

    def woload(e, c, os_):
        par = "w" + str(os_ % 2)
        e.dma_start(out=wot[os_ % 2][:].rearrange("p (dc o) -> p dc o", o=512),
                    in_=c.d["WoutT"].rearrange("(dc p) o -> p dc o", p=128)
                    [:, :, os_ * 512:(os_ + 1) * 512]).then_inc(c.S(par), 16)
        c.bump(par, 16)
        return _mark(e)

    c.step(("gpsimd", lambda e, c: woload(e, c, 0)))
    c.step(("gpsimd", lambda e, c: woload(e, c, 1)))
    for os_ in range(8):
        for tg in range(2):
            def womm(e, c, os_=os_, tg=tg):
                if tg == 0:
                    wdma(e, c, "w" + str(os_ % 2))
                r = None
                for i in range(4):
                    tc = tg * 4 + i
                    ps = (ps_a, ps_b, ps_bc1, ps_bc2)[i]
                    for dc in range(NDC):
                        e.matmul(ps[:, 0:512],
                                 hb[:, dc * T + tc * 128: dc * T + (tc + 1) * 128],
                                 wot[os_ % 2][:, dc * 512:(dc + 1) * 512],
                                 start=(dc == 0), stop=False)
                    r = e.matmul(ps[:, 0:512], onesr[:],
                                 boutr[:, os_ * 512:(os_ + 1) * 512],
                                 start=False, stop=True)
                return r

            c.step(("tensor", womm))

            def woev(e, c, os_=os_, tg=tg):
                r = None
                for i in range(4):
                    tc = tg * 4 + i
                    ps = (ps_a, ps_b, ps_bc1, ps_bc2)[i]
                    r = e.tensor_copy(
                        s1[:, tc * EMB + os_ * 512: tc * EMB + (os_ + 1) * 512],
                        ps[:, 0:512])
                return r

            wparts = [("vector", woev)]
            if tg == 1 and os_ + 2 < 8:
                wparts.append(("gpsimd", lambda e, c, os_=os_: woload(e, c, os_ + 2)))
            c.step(*wparts)

    if not FUSED:
        def s1store(e, c):
            for tc in range(8):
                e.dma_start(out=c.d["s1_d"][tc * 128:(tc + 1) * 128, :],
                            in_=s1[:, tc * EMB:(tc + 1) * EMB]).then_inc(c.S("so"), 16)
                c.bump("so", 16)
            wdma(e, c, "so")
            return _mark(e)

        c.step(("gpsimd", s1store))

    c.free("g5_wot0", "g5_wot1", "g5_h", "g5_hb", "g5_row", "g5_rowb",
           "g5_urho", "g5_dini", "g5_tails", "g5_recv", "g5_zero",
           "g5_pstage", *(("g5_pp_" + nm) for nm in pnames),
           "g5_id", "g5_idF", "g5_onesc", "g5_onesr", "g5_is2", "g5_isA",
           "g5_boutr", "g5_psa", "g5_psb", "g5_psbc1", "g5_psbc2", "g5_pspt")


def build_g6(c):
    nc = c.nc
    if not FUSED:
        c.t["s1"] = c.sb("s1", [128, 8 * EMB], BF16, side="right")
    c.t["xnT"] = c.sb("xnT_g6", [128, NEC * T], BF16, side="right")
    load_xnT(c, "xn1_bf")
    if not FUSED:
        def s1load(e, c):
            for tc in range(8):
                e.dma_start(out=c.t["s1"][:, tc * EMB:(tc + 1) * EMB],
                            in_=c.d["s1_d"][tc * 128:(tc + 1) * 128, :]
                            ).then_inc(c.S("pl"), 16)
                c.bump("pl", 16)
            wdma(e, c, "pl")
            return _mark(e)

        c.step(("gpsimd", s1load))
    xnT = c.t["xnT"]
    s1 = c.t["s1"]

    onesr = c.sb("g6_or", [1, 128], BF16)
    ballp = c.sb("g6_ballp", [128, EMB], BF16)
    bbr = c.sb("g6_bbr", [1, EMB], BF16)
    hw = c.sb("g6_hw", [128, T], BF16)
    ident = c.sb("g6_id", [128, 128], BF16)
    identF = c.sb("g6_idF", [128, 128], F32)
    sel = c.sb("g6_sel", [8, 128], BF16)
    id_d = nc.inline_tensor(np.eye(128, dtype=BF), name="g6ident").ap()
    idF_d = nc.inline_tensor(np.eye(128, dtype=np.float32), name="g6idF").ap()
    or_d = nc.inline_tensor(np.ones((1, 128), dtype=BF), name="g6ones").ap()
    sel_d = nc.inline_tensor(np.kron(np.eye(8), np.ones((1, 16))).astype(BF),
                             name="g6sel").ap()
    wgt = c.sb("g6_wgt", [128, NEC * 8], BF16)
    aall = c.sb("g6_aall", [128, NEC * 128], BF16)
    lgF = c.sb("g6_lgF", [8, T], F32)
    lgt = c.sb("g6_lgt", [128, 8], F32)
    lg2 = c.sb("g6_lg2", [128, 8], F32)
    m1 = c.sb("g6_m1", [128, 4], F32)
    route = c.sb("g6_route", [128, 8], F32)
    routeb = c.sb("g6_routeb", [128, 8], BF16)
    routeT = c.sb("g6_routeT", [8, T], BF16)
    hall = c.sb("g6_hall", [128, T], BF16)
    ps_a = c.ps("g6_psa", [128, 1024])
    ps_b = c.ps("g6_psb", [128, 1024])
    ps_t = c.ps("g6_pst", [128, 128])
    ps_tb = c.ps("g6_pstb", [8, 128], BF16)

    def prolog(e, c):
        for src, dst in ((id_d, ident), (idF_d, identF), (or_d, onesr),
                         (sel_d, sel), (c.d["bb_row"], bbr)):
            e.dma_start(out=dst[:], in_=src).then_inc(c.S("pl"), 16)
            c.bump("pl", 16)
        e.dma_start(out=wgt[:].rearrange("p (ec x) -> p ec x", x=8),
                    in_=c.d["WgT"].rearrange("(ec p) x -> p ec x", p=128)
                    ).then_inc(c.S("pl"), 16)
        e.dma_start(out=aall[:].rearrange("p (ec x) -> p ec x", x=128),
                    in_=c.d["AallT"].rearrange("(ec p) x -> p ec x", p=128)
                    ).then_inc(c.S("pl"), 16)
        e.dma_start(out=ballp[:], in_=c.d["BallP"]).then_inc(c.S("pl"), 16)
        c.bump("pl", 48)
        return _mark(e)

    c.step(("gpsimd", prolog))

    def lgmm(e, c):
        wdma(e, c, "pl")
        r = None
        for th in range(2):
            for ec in range(NEC):
                r = e.matmul(ps_a[0:8, th * 512:(th + 1) * 512],
                             wgt[:, ec * 8:(ec + 1) * 8],
                             xnT[:, ec * T + th * 512: ec * T + (th + 1) * 512],
                             start=(ec == 0), stop=(ec == NEC - 1))
        return r

    c.step(("tensor", lgmm))
    c.step(("vector", lambda e, c: VG(e, c, e.tensor_copy(lgF[:], ps_a[0:8, :])) and None))

    def hallmm(e, c):
        r = None
        for th in range(2):
            for ec in range(NEC):
                r = e.matmul(ps_b[:, th * 512:(th + 1) * 512],
                             aall[:, ec * 128:(ec + 1) * 128],
                             xnT[:, ec * T + th * 512: ec * T + (th + 1) * 512],
                             start=(ec == 0), stop=(ec == NEC - 1))
        return r

    c.step(("tensor", hallmm))
    c.step(("vector", lambda e, c: e.tensor_copy(hall[:], ps_b[:])))

    for tc in range(8):
        c.step(("tensor", lambda e, c, tc=tc:
                e.transpose(ps_t[:, 0:8], lgF[:, tc * 128:(tc + 1) * 128],
                            identF[0:8, 0:8])))

        def top2(e, c, tc=tc):
            VG(e, c, e.tensor_copy(lgt[:], ps_t[:, 0:8]))
            VG(e, c, e.tensor_reduce(m1[:, 0:1], lgt[:], AX.X, ALU.max), extra=True)
            VG(e, c, e.tensor_reduce(m1[:, 1:2], lgt[:], AX.X, ALU.max, negate=True),
               extra=True)
            VG(e, c, e.tensor_tensor(lg2[:], lgt[:],
                                     m1[:, 0:1].broadcast_to((128, 8)), ALU.is_ge))
            VG(e, c, e.tensor_scalar(lg2[:], lg2[:], 1e30, None, ALU.mult))
            VG(e, c, e.tensor_tensor(lg2[:], lgt[:], lg2[:], ALU.subtract))
            VG(e, c, e.tensor_reduce(m1[:, 2:3], lg2[:], AX.X, ALU.max), extra=True)
            return None

        c.step(("vector", top2))
        c.step(("scalar", lambda e, c: VG(e, c, e.activation(lg2[:], lgt[:], AF.Exp,
                                                              bias=m1[:, 1:2])) and None))

        def routev(e, c, tc=tc):
            VG(e, c, e.tensor_tensor(route[:], lgt[:],
                                     m1[:, 2:3].broadcast_to((128, 8)), ALU.is_ge))
            VG(e, c, e.tensor_tensor(route[:], route[:], lg2[:], ALU.mult))
            VG(e, c, e.tensor_reduce(m1[:, 3:4], route[:], AX.X, ALU.add), extra=True)
            VG(e, c, e.reciprocal(m1[:, 3:4], m1[:, 3:4]), extra=True)
            VG(e, c, e.tensor_scalar(route[:], route[:], m1[:, 3:4], None, ALU.mult))
            VG(e, c, e.tensor_copy(routeb[:], route[:]))
            return None

        c.step(("vector", routev))
        c.step(("tensor", lambda e, c: e.transpose(ps_tb[:, 0:128], routeb[:],
                                                   ident[:])))
        c.step(("vector", lambda e, c, tc=tc:
                VG(e, c, e.tensor_copy(routeT[:, tc * 128:(tc + 1) * 128],
                                       ps_tb[0:8, :])) and None))

    def selmm(e, c):
        e.matmul(ps_a[:, 0:512], sel[:], routeT[:, 0:512], start=True, stop=True)
        return e.matmul(ps_a[:, 512:1024], sel[:], routeT[:, 512:1024],
                        start=True, stop=True)

    c.step(("tensor", selmm))
    c.step(("vector", lambda e, c: e.tensor_tensor(hw[:], hall[:], ps_a[:], ALU.mult)))
    c.free("g6_psa", "g6_psb", "g6_pst", "g6_pstb", "g6_lgF", "g6_lgt",
           "g6_lg2", "g6_m1", "g6_route", "g6_routeb", "g6_routeT", "g6_hall",
           "g6_wgt", "g6_aall", "g6_id", "g6_idF", "g6_sel")

    wbt = [c.sb(f"g6_wbt{i}", [128, 1024], BF16) for i in range(3)]
    silu_t = [c.sb(f"g6_silu{i}", [128, 512], BF16) for i in range(2)]
    sg = c.sb("g6_sg", [128, 512], F32)
    xts = [c.sb(f"g6_xt{i}", [128, 512], F32) for i in range(3)]
    out_t = [c.sb(f"g6_out{i}", [128, 512], F32) for i in range(2)]
    ps8 = [c.ps(f"g6_ps{i}", [128, 512]) for i in range(8)]

    NEP = NEC // 2

    def wbload(e, c, j):
        os_, ep_ = j // NEP, j % NEP
        slot = j % 3
        par = "w" + str(j % 2)
        e.dma_start(out=wbt[slot][:].rearrange("p (cc n) -> p cc n", n=512),
                    in_=c.d["WbT"][os_, ep_ * 256:(ep_ + 1) * 256, :]
                    .rearrange("(cc p) n -> p cc n", p=128)).then_inc(c.S(par), 16)
        c.bump(par, 16)
        return _mark(e)

    c.step(("gpsimd", lambda e, c: wbload(e, c, 0)))
    c.step(("gpsimd", lambda e, c: wbload(e, c, 1)))
    for os_ in range(8):
        for ep_ in range(NEP):
            j = os_ * NEP + ep_

            def mm(e, c, j=j, ep_=ep_):
                slot = j % 3
                wdma(e, c, "w" + str(j % 2))
                r = None
                for half in range(2):
                    ec = ep_ * 2 + half
                    for tc in range(8):
                        r = e.matmul(ps8[tc][:],
                                     xnT[:, ec * T + tc * 128: ec * T + (tc + 1) * 128],
                                     wbt[slot][:, half * 512:(half + 1) * 512],
                                     start=(ec == 0), stop=False)
                return r

            parts = [("tensor", mm)]
            if j + 2 < 8 * NEP:
                parts.append(("gpsimd", lambda e, c, j=j: wbload(e, c, j + 2)))
            c.step(*parts)

        def dmm(e, c, os_=os_):
            r = None
            for tc in range(8):
                e.matmul(ps8[tc][:], hw[:, tc * 128:(tc + 1) * 128],
                         ballp[:, os_ * 512:(os_ + 1) * 512], start=False, stop=False)
                r = e.matmul(ps8[tc][:], onesr[:], bbr[:, os_ * 512:(os_ + 1) * 512],
                             start=False, stop=True)
            return r

        c.step(("tensor", dmm))

        def xload(e, c, os_, tc):
            par = "x" + str(tc % 2)
            e.dma_start(out=xts[tc % 3][:],
                        in_=c.d["xnew_d"][tc * 128:(tc + 1) * 128,
                                          os_ * 512:(os_ + 1) * 512]
                        ).then_inc(c.S(par), 16)
            c.bump(par, 16)
            return _mark(e)

        def sil(e, c, tc):
            return e.activation(silu_t[tc % 2][:], ps8[tc][:], AF.Silu)

        def ep(e, c, os_, tc):
            wdma(e, c, "x" + str(tc % 2))
            e.tensor_tensor(sg[:], silu_t[tc % 2][:],
                            s1[:, tc * EMB + os_ * 512: tc * EMB + (os_ + 1) * 512],
                            ALU.mult)
            wdma(e, c, "s" + str(tc % 2))
            return e.tensor_tensor(out_t[tc % 2][:], xts[tc % 3][:], sg[:], ALU.add)

        def ostore(e, c, os_, tc):
            par = "s" + str(tc % 2)
            e.dma_start(out=c.d["out"][tc * 128:(tc + 1) * 128,
                                       os_ * 512:(os_ + 1) * 512],
                        in_=out_t[tc % 2][:]).then_inc(c.S(par), 16)
            c.bump(par, 16)
            return _mark(e)

        c.step(("sync", lambda e, c, os_=os_: xload(e, c, os_, 0)))
        c.step(("sync", lambda e, c, os_=os_: xload(e, c, os_, 1)))
        for tc in range(10):
            parts = []
            if tc < 8:
                parts.append(("scalar", lambda e, c, tc=tc: sil(e, c, tc)))
                if 2 <= tc + 1 < 8:
                    parts.append(("sync", lambda e, c, os_=os_, tc=tc:
                                  xload(e, c, os_, tc + 1)))
            if 0 <= tc - 1 < 8:
                parts.append(("vector", lambda e, c, os_=os_, tc=tc: ep(e, c, os_, tc - 1)))
            if 0 <= tc - 2 < 8:
                parts.append(("gpsimd", lambda e, c, os_=os_, tc=tc: ostore(e, c, os_, tc - 2)))
            c.step(*parts)

    def fin(e, c):
        wdma(e, c, "s0")
        wdma(e, c, "s1")
        return _mark(e)

    c.step(("gpsimd", fin))


BUILDERS = {"g1": build_g1, "g2": build_g2, "g3": build_g3, "g4": build_g4,
            "g5": build_g5, "g6": build_g6}
PERSIST = {
    "g1": [("xnT", [128, NEC * T], BF16)],
    "g2": [("s0T", [128, 8 * 1024], BF16)],
    "g3": [("xnT", [128, NEC * T], BF16)],
    "g4": [("s0T", [128, 8 * 1024], BF16)],
    "g5": [("xnT", [128, NEC * T], BF16)],
    "g6": [],
}


def build_program(groups, host, fused):
    global FUSED
    FUSED = fused
    nc = bass.Bass("TRN2", target_bir_lowering=False)
    c = Ctx(nc)
    decl_io(c, groups, host, fused)
    for g in groups:
        for nm, shape, dt in PERSIST[g]:
            if nm not in c.t:
                c.t[nm] = c.sb(nm, shape, dt, side="right")
    with nc.allow_low_precision(reason="bf16 matmul operands by design"), \
            nc.Block() as block:
        c.block = block
        for g in groups:
            BUILDERS[g](c)
    return c


def _in_map(groups, host, core):
    m = {}
    for g in groups:
        for k in G_INPUTS[g]:
            m.setdefault(k, host[k])
        if "x" in G_READS[g]:
            m["x"] = host["x"][core]
        if g == "g5":
            m["is2"] = host["is2"][core]
    return m


def kernel(**inputs):
    host = host_prep(inputs)
    if FUSED:
        groups = [g for gs in GROUP_SETS for g in gs]
        c = build_program(groups, host, fused=True)
        in_maps = [_in_map(groups, host, core) for core in range(8)]
        res = run_bass_kernel_spmd(c.nc, in_maps, core_ids=list(range(8)))
        outs = [np.asarray(res.results[core]["out"]) for core in range(8)]
    else:
        handoff = [dict() for _ in range(8)]
        outs = [None] * 8
        for gs in GROUP_SETS:
            c = build_program(gs, host, fused=False)
            reads, writes = set(), set()
            for g in gs:
                reads |= set(G_READS[g]) - {"x"}
                writes |= set(G_WRITES[g])
            reads -= writes
            in_maps = []
            for core in range(8):
                m = _in_map(gs, host, core)
                for nm in reads:
                    m[nm] = handoff[core][nm]
                in_maps.append(m)
            res = run_bass_kernel_spmd(c.nc, in_maps, core_ids=list(range(8)))
            for core in range(8):
                for nm in writes:
                    handoff[core][nm] = np.asarray(res.results[core][nm])
                if "g6" in gs:
                    outs[core] = np.asarray(res.results[core]["out"])
    full = np.empty((4, 2048, EMB), np.float32)
    for core in range(8):
        b, hh = core // 2, core % 2
        full[b, hh * T:(hh + 1) * T, :] = outs[core]
    return full
